# version 07
# baseline (speedup 1.0000x reference)
import contextlib
import numpy as np
import ml_dtypes
import concourse.bass as bass
import concourse.mybir as mybir
from concourse.bass_utils import run_bass_kernel_spmd

F32 = mybir.dt.float32
BF16 = mybir.dt.bfloat16
I32 = mybir.dt.int32
AF = mybir.ActivationFunctionType
ALU = mybir.AluOpType
AX = mybir.AxisListType

D = 2048
SEQ = 8192
NC = 8
KB = 1024
PHASES = ("ln1", "qkv", "mh", "attn", "exch", "wo", "ln2", "ffn1", "ffn2", "ffn3")


class P:
    def __init__(self, nc, st):
        self.nc, self.st = nc, st
        self.sem, self.cnt = {}, {}

    def mksem(self, name):
        self.sem[name] = self.st.enter_context(self.nc.semaphore(name))
        self.cnt[name] = 0

    def inc(self, instr, name, n=1):
        instr.then_inc(self.sem[name], n)
        self.cnt[name] += n
        return self.cnt[name]

    def wait(self, eng, name, val):
        if val > 0:
            eng.wait_ge(self.sem[name], val)


def _nops(eng, n=6):
    for _ in range(n):
        eng.engine_nop()


class Ring:

    def __init__(self, p, name, w, group, sb):
        self.p, self.name, self.w, self.group, self.sb = p, name, w, group, sb
        self.gw = w * group
        self.nload = 0
        self.ncons = 0
        self.marks = []
        for s in range(3):
            p.mksem(f"l{name}{s}")

    def mark(self, sem, val):
        self.marks.append((sem, val))

    def load(self, g, dram_ap):
        m = self.nload
        if m >= 3:
            sem, val = self.marks[m - 3]
            self.p.wait(g, sem, val)
        s = m % 3
        slot = self.sb[:, s * self.gw:(s + 1) * self.gw]
        self.p.inc(g.dma_start(out=slot, in_=dram_ap), f"l{self.name}{s}", 16)
        self.nload += 1

    def use(self, eng):
        k = self.ncons
        m, i = k // self.group, k % self.group
        if i == 0:
            self.p.wait(eng, f"l{self.name}{m % 3}", 16 * (m // 3 + 1))
        base = (m % 3) * self.gw + i * self.w
        self.ncons += 1
        return self.sb[:, base:base + self.w]


def build_program(upto="ffn3", debug=()):
    sub = None
    if upto in ("qkva", "qkvb"):
        sub, upto = upto, "qkv"
    nc = bass.Bass("TRN2", target_bir_lowering=False, debug=False,
                   detect_race_conditions=False)
    nup = PHASES.index(upto)

    def on(ph):
        return PHASES.index(ph) <= nup

    xq = nc.dram_tensor("xq", [1024, D], F32, kind="ExternalInput")
    xr = nc.dram_tensor("xr", [1024, D], F32, kind="ExternalInput")
    cid = nc.dram_tensor("cid", [1, 1], I32, kind="ExternalInput")
    wq_g = nc.dram_tensor("wq_g", [32, 128, 1024], F32, kind="ExternalInput")
    wk_g = nc.dram_tensor("wk_g", [32, 128, 1024], F32, kind="ExternalInput")
    wv_g = nc.dram_tensor("wv_g", [32, 128, 1024], F32, kind="ExternalInput")
    wo_g = nc.dram_tensor("wo_g", [32, 128, 1024], F32, kind="ExternalInput")
    w1_g = nc.dram_tensor("w1_g", [64, 128, 1024], F32, kind="ExternalInput")
    w2_g = nc.dram_tensor("w2_g", [128, 128, 1024], F32, kind="ExternalInput")
    w3_g = nc.dram_tensor("w3_g", [64, 128, 1024], F32, kind="ExternalInput")
    b1d = nc.dram_tensor("b1d", [128, 32], F32, kind="ExternalInput")
    b2d = nc.dram_tensor("b2d", [128, 32], F32, kind="ExternalInput")
    b3d = nc.dram_tensor("b3d", [128, D], F32, kind="ExternalInput")
    y = nc.dram_tensor("y", [1024, D], F32, kind="ExternalOutput")
    ag_in0 = nc.dram_tensor("ag_in0", [128, SEQ], BF16)
    ag_in1 = nc.dram_tensor("ag_in1", [128, SEQ], BF16)
    ag_out0 = nc.dram_tensor("ag_out0", [NC * 128, SEQ], BF16)
    ag_out1 = nc.dram_tensor("ag_out1", [NC * 128, SEQ], BF16)
    x1s = nc.dram_tensor("x1s", [1024, D], F32)
    dbg_shapes = {"hT": [128, 16 * KB], "QT": [128, 16 * KB],
                  "KT": [128, 16 * KB], "V": [128, 8 * D],
                  "GT": [128, 16 * KB], "h2T": [128, 16 * KB],
                  "x1": [128, 16 * KB], "z1T": [128, 32 * KB],
                  "z2T": [128, 32 * KB], "agin": [128, 2 * SEQ],
                  "mh": [128, 1]}
    dbg = {k: nc.dram_tensor("dbg_" + k, dbg_shapes[k], F32, kind="ExternalOutput")
           for k in debug}

    st = contextlib.ExitStack()
    with st:
        SB_BASE = 20480

        def place(name, shape, dtype, off):
            return nc.alloc_sbuf_tensor_at(name, shape, dtype, offset=SB_BASE + off)

        HT = place("HT", [128, 16 * KB], BF16, 0)
        QT = place("QT", [128, 16 * KB], BF16, 32768)
        KT = place("KT", [128, 16 * KB], BF16, 65536)
        X1 = place("X1", [128, 16 * KB], F32, 32768)
        Z1 = place("Z1", [128, 32 * KB], BF16, 32768)
        VSB = place("VSB", [128, 8 * D], BF16, 98304)
        Z2 = place("Z2", [128, 32 * KB], BF16, 98304)
        GT = place("GT", [128, 16 * KB], BF16, 131072)
        MSK = place("MSK", [128, 5 * 512], BF16, 131072)
        HSB = place("HSB", [128, D], BF16, 136192)
        HSB2 = place("HSB2", [128, D], BF16, 140288)
        PT = place("PT", [128, 2 * KB], BF16, 144384)
        AGS = place("AGS", [128, 2 * 512], BF16, 148480)
        RCB = place("RCB", [128, 512], F32, 150528)
        LNT = place("LNT", [128, 512], F32, 152576)
        XST = place("XST", [128, 2 * D], F32, 163840)
        X1R = place("X1R", [128, 2 * 512], F32, 180224)
        B3S = place("B3S", [128, 2 * 512], F32, 184320)
        RA = place("RA", [128, 3 * 8 * 128], BF16, 188416)
        RB = place("RB", [128, 3 * 2 * 512], BF16, 194560)
        SM = place("SM", [128, 64], F32, 200704)
        SMB = place("SMB", [128, 16], BF16, 200960)
        IDNB = place("IDNB", [128, 128], BF16, 201024)
        ONES = place("ONES", [128, 128], BF16, 201536)
        B1S = place("B1S", [128, 32], F32, 201792)
        B2S = place("B2S", [128, 32], F32, 201920)
        MH = place("MH", [128, 1], F32, 202048)
        CIDS = place("CIDS", [1, 1], I32, 202080)
        ONESF = place("ONESF", [128, 128], F32, 202112)
        X1T0 = place("X1T0", [128, 512], F32, 202624)
        X1T1 = place("X1T1", [128, 512], F32, 204672)
        X1T2 = place("X1T2", [128, 512], F32, 206720)
        YST = XST

        psA = st.enter_context(nc.psum_tensor("psA", [128, 4096], F32))

        def bank(b):
            return psA[:, 512 * b:512 * (b + 1)]

        def bankb(b):
            return psA[:, 512 * b:512 * (b + 1)].bitcast(BF16)

        p = P(nc, st)
        for s in ["dX0", "dX1", "dB0", "dB1", "dC", "dG", "dSP", "dYY",
                  "dYA00", "dYA01", "dYA10", "dYA11",
                  "sPE", "sACT", "sDVE", "eDVE", "eACT",
                  "aS", "aE", "aM", "aP", "aL", "aR", "aV", "aB", "cCC", "rA", "rB",
                  "dY0", "dY1", "dR0", "dR1", "dR2", "dR3",
                  "dR4", "dR5", "dR6", "dR7", "dW0", "dW1"]:
            p.mksem(s)
        ringA = Ring(p, "A", 128, 8, RA)
        ringB = Ring(p, "B", 512, 2, RB)

        block = st.enter_context(nc.Block())

        att_blocks = [(h, g, b) for h in range(2) for g in range(4) for b in range(4)]
        wins = []
        for bi, (h, g, b) in enumerate(att_blocks):
            tl = [(kj, kr) for kj in range(16) for kr in range(b + 1)]
            for w in range(len(tl) // 2):
                wins.append(dict(bi=bi, h=h, g=g, b=b,
                                 tiles=(tl[2 * w], tl[2 * w + 1]),
                                 first=(w == 0), last=(w == len(tl) // 2 - 1)))
        NWIN = len(wins)
        blk_lastwin = {}
        for wi, w in enumerate(wins):
            blk_lastwin[w["bi"]] = wi

        def win_masked(w):
            return any(kr == w["b"] for kj, kr in w["tiles"])

        nmask_upto = []
        c_ = 0
        for w in wins:
            c_ += 1 if win_masked(w) else 0
            nmask_upto.append(c_)

        @block.gpsimd
        def _(g):
            g.memset(ONES[:], 1.0)
            g.memset(ONESF[:], 1.0)
            g.memset(SMB[:], 1.0)
            g.memset(SM[:], 0.0)
            g.iota(IDNB[:], [[1, 128]], base=0, channel_multiplier=-1,
                   allow_small_or_imprecise_dtypes=True)
            instr = None
            for r in range(5):
                for j in range(4):
                    instr = g.iota(MSK[:, r * 512 + j * 128:r * 512 + (j + 1) * 128],
                                   [[1, 128]], base=(0 if j >= r else -1),
                                   channel_multiplier=-1,
                                   allow_small_or_imprecise_dtypes=True)
            p.inc(instr, "sDVE")
            p.inc(g.dma_start(out=CIDS[:], in_=cid.ap()), "dG", 16)

        setup_sDVE = p.cnt["sDVE"]

        @block.vector
        def _(v):
            p.wait(v, "sDVE", setup_sDVE)
            for r in range(5):
                v.tensor_scalar(MSK[:, r * 512:(r + 1) * 512],
                                MSK[:, r * 512:(r + 1) * 512], 0.0, None, ALU.is_ge)
            p.inc(v.tensor_scalar(IDNB[:], IDNB[:], 0.0, None, ALU.is_equal), "sDVE")

        masks_sDVE = p.cnt["sDVE"]

        def emit_ln_tile(k, src_ap, ht_tokbase, W):
            HSBk = HSB if k % 2 == 0 else HSB2
            """Emit LN + transpose of one 128-token tile into HT (bf16).

            W (absolute semaphore values):
              src_waits: [(sem, val)] before touching src (scalar AND vector)
              sq_guard:  eDVE value guarding SM parity slots (0 = skip)
              sACT0:     sACT count before this tile's two incs
              sDVE_val:  sDVE count after this tile's var inc
              norm_eDVE: eDVE count after this tile's normalize inc
              pe_round_waits: 4 absolute eDVE values (0 = skip) per PE round
              sPE0:      sPE count before this tile's 4 round incs
            Emits: sACT +2, sDVE +1, eDVE +5 (normalize + 4 copy rounds),
            sPE +4.
            """
            pr = (k % 2) * 16

            @block.scalar
            def _(sc):
                for sem, val in W["src_waits"]:
                    p.wait(sc, sem, val)
                if W["sq_guard"]:
                    p.wait(sc, "eDVE", W["sq_guard"])
                p.wait(sc, "sPE", max(W["sPE0"] - 4, 0))
                nv = p.inc(sc.activation(HSBk[:], src_ap, AF.Square,
                                         accum_out=SM[:, pr + 1:pr + 2]), "sACT")
                assert nv == W["sACT0"] + 1

            @block.vector
            def _(v):
                for sem, val in W["src_waits"]:
                    p.wait(v, sem, val)
                if W["sq_guard"]:
                    p.wait(v, "eDVE", W["sq_guard"])
                v.tensor_reduce(SM[:, pr + 0:pr + 1], src_ap, AX.X, ALU.add)
                p.wait(v, "sACT", W["sACT0"] + 1)
                _nops(v)
                v.tensor_scalar(SM[:, pr + 3:pr + 4], SM[:, pr + 0:pr + 1],
                                -1.0 / D, None, ALU.mult)
                _nops(v)
                v.tensor_tensor(SM[:, pr + 4:pr + 5], SM[:, pr + 3:pr + 4],
                                SM[:, pr + 3:pr + 4], ALU.mult)
                v.tensor_scalar(SM[:, pr + 5:pr + 6], SM[:, pr + 1:pr + 2],
                                1.0 / D, None, ALU.mult)
                _nops(v)
                v.tensor_tensor(SM[:, pr + 6:pr + 7], SM[:, pr + 5:pr + 6],
                                SM[:, pr + 4:pr + 5], ALU.subtract)
                _nops(v)
                nv = p.inc(v.tensor_scalar(SM[:, pr + 9:pr + 10],
                                           SM[:, pr + 6:pr + 7], 1e-5, None,
                                           ALU.add), "sDVE")
                assert nv == W["sDVE_val"]

            @block.scalar
            def _(sc):
                p.wait(sc, "sDVE", W["sDVE_val"])
                sc.activation(SM[:, pr + 7:pr + 8], SM[:, pr + 9:pr + 10], AF.Ln)
                nv = p.inc(sc.activation(SM[:, pr + 8:pr + 9], SM[:, pr + 7:pr + 8],
                                         AF.Exp, scale=-0.5), "sACT")
                assert nv == W["sACT0"] + 2

            @block.vector
            def _(v):
                p.wait(v, "sACT", W["sACT0"] + 2)
                _nops(v)
                nv = p.inc(v.tensor_scalar(HSBk[:], src_ap, SM[:, pr + 3:pr + 4],
                                           SM[:, pr + 8:pr + 9], ALU.add,
                                           ALU.mult), "eDVE")
                assert nv == W["norm_eDVE"]

            @block.tensor
            def _(t):
                p.wait(t, "sDVE", masks_sDVE)
                p.wait(t, "eDVE", W["norm_eDVE"])
                for r in range(4):
                    if W["pe_round_waits"][r]:
                        p.wait(t, "eDVE", W["pe_round_waits"][r])
                    for s4 in range(4):
                        jj = 4 * r + s4
                        b = 4 * (r % 2) + s4
                        instr = t.transpose(bankb(b)[:, 0:128],
                                            HSBk[:, 128 * jj:128 * (jj + 1)], IDNB[:])
                    nv = p.inc(instr, "sPE")
                    assert nv == W["sPE0"] + r + 1


        def emit_ln_copies(k, ht_tokbase, W):
            @block.vector
            def _(v):
                for r in range(4):
                    p.wait(v, "sPE", W["sPE0"] + r + 1)
                    for s4 in range(4):
                        jj = 4 * r + s4
                        b = 4 * (r % 2) + s4
                        instr = v.tensor_copy(
                            HT[:, jj * KB + ht_tokbase:jj * KB + ht_tokbase + 128],
                            bankb(b)[:, 0:128])
                    nv = p.inc(instr, "eDVE")
                    assert nv == W["copies_eDVE"] + r + 1

        b_ln1 = {k: p.cnt[k] for k in ("eDVE", "sACT", "sDVE", "sPE")}
        nX = [0, 0]

        @block.sync
        def _(s):
            p.inc(s.dma_start(out=B1S[:], in_=b1d.ap()), "dC", 16)
            p.inc(s.dma_start(out=B2S[:], in_=b2d.ap()), "dC", 16)
            for k in range(8):
                if k >= 2:
                    p.wait(s, "eDVE", b_ln1["eDVE"] + 5 * (k - 2) + 1)
                nX[k % 2] += 1
                p.inc(s.dma_start(out=XST[:, (k % 2) * D:(k % 2 + 1) * D],
                                  in_=xq.ap()[128 * k:128 * (k + 1), :]),
                      "dX%d" % (k % 2), 16)

        if on("ln1"):
            e0, s0 = b_ln1["eDVE"], b_ln1["sPE"]
            for k in range(8):
                if k >= 1:
                    emit_ln_copies(k - 1, 128 * (k - 1),
                                   dict(sPE0=s0 + 4 * (k - 1),
                                        copies_eDVE=e0 + 5 * (k - 1) + 1))
                emit_ln_tile(
                    k, XST[:, (k % 2) * D:(k % 2 + 1) * D], 128 * k,
                    dict(src_waits=[("dX%d" % (k % 2), 16 * (k // 2 + 1))],
                         sq_guard=(e0 + 5 * (k - 2) + 1) if k >= 2 else 0,
                         sACT0=b_ln1["sACT"] + 2 * k,
                         sDVE_val=b_ln1["sDVE"] + k + 1,
                         norm_eDVE=e0 + 5 * k + 1,
                         pe_round_waits=[(e0 + 5 * k - 1) if k >= 1 else 0,
                                         (e0 + 5 * k) if k >= 1 else 0,
                                         e0 + 5 * k + 2,
                                         e0 + 5 * k + 3],
                         sPE0=s0 + 4 * k))
            emit_ln_copies(7, 128 * 7, dict(sPE0=s0 + 28, copies_eDVE=e0 + 36))
        ln1_eDVE = p.cnt["eDVE"]

        if on("qkv"):
            def proj_qk(dst, wsrc):
                b0 = {k: p.cnt[k] for k in ("sPE", "eDVE", "rA")}
                for j in range(32):
                    if j % 2 == 0:
                        ringA.mark("rA", b0["rA"] + j // 2 + 1)
                    else:
                        ringA.mark("sPE", b0["sPE"] + j // 2 + 1)

                @block.gpsimd
                def _(g):
                    for t in range(32):
                        ringA.load(g, wsrc.ap()[t])

                @block.tensor
                def _(t):
                    p.wait(t, "eDVE", ln1_eDVE)
                    for dq in range(16):
                        pr2 = dq % 2
                        if dq >= 2:
                            p.wait(t, "eDVE", b0["eDVE"] + dq - 1)
                        for kc in range(16):
                            w = ringA.use(t)
                            t.matmul(bank(4 + 2 * pr2), w, HT[:, kc * KB:kc * KB + 512],
                                     start=(kc == 0), stop=(kc == 15))
                            instr = t.matmul(bank(5 + 2 * pr2), w,
                                             HT[:, kc * KB + 512:(kc + 1) * KB],
                                             start=(kc == 0), stop=(kc == 15))
                            if kc == 7:
                                p.inc(instr, "rA")
                        p.inc(instr, "sPE")

                @block.vector
                def _(v):
                    for dq in range(16):
                        pr2 = dq % 2
                        p.wait(v, "sPE", b0["sPE"] + dq + 1)
                        v.tensor_copy(dst[:, dq * KB:dq * KB + 512], bank(4 + 2 * pr2))
                        p.inc(v.tensor_copy(dst[:, dq * KB + 512:(dq + 1) * KB],
                                            bank(5 + 2 * pr2)), "eDVE")

            proj_qk(QT, wq_g)
            if sub != "qkva":
                proj_qk(KT, wk_g)
            qk_eDVE = p.cnt["eDVE"]
            bV = {k: p.cnt[k] for k in ("sPE", "eDVE", "rB")}
            if sub not in ("qkva", "qkvb"):
                for j in range(32):
                    dv, pos = j // 8, j % 8
                    if pos < 7:
                        ringB.mark("rB", bV["rB"] + dv * 7 + pos + 1)
                    else:
                        ringB.mark("sPE", bV["sPE"] + dv + 1)

                @block.gpsimd
                def _(g):
                    for t in range(32):
                        ringB.load(g, wv_g.ap()[t])

                @block.tensor
                def _(t):
                    p.wait(t, "eDVE", qk_eDVE)
                    for dv in range(4):
                        if dv >= 1:
                            p.wait(t, "eDVE", bV["eDVE"] + dv)
                        for kc in range(16):
                            w = ringB.use(t)
                            for tv in range(8):
                                instr = t.matmul(
                                    bank(tv),
                                    HT[:, kc * KB + 128 * tv:kc * KB + 128 * (tv + 1)],
                                    w, start=(kc == 0), stop=(kc == 15))
                            if kc % 2 == 1 and kc != 15:
                                p.inc(instr, "rB")
                        p.inc(instr, "sPE")

                @block.vector
                def _(v):
                    for dv in range(4):
                        p.wait(v, "sPE", bV["sPE"] + dv + 1)
                        for tv in range(8):
                            instr = v.tensor_copy(
                                VSB[:, tv * D + dv * 512:tv * D + (dv + 1) * 512],
                                bank(tv))
                        p.inc(instr, "eDVE")

        v_eDVE = p.cnt["eDVE"]

        if on("mh"):
            bM = {k: p.cnt[k] for k in ("sDVE", "sPE", "eDVE", "sACT")}

            for m in range(32):
                mat = QT if m < 16 else KT
                ch = m % 16
                run = 40 if m < 16 else 41
                sl = (m % 2) * KB

                @block.vector
                def _(v, m=m, mat=mat, ch=ch, sl=sl):
                    if m == 0:
                        p.wait(v, "eDVE", v_eDVE)
                    if m >= 2:
                        p.wait(v, "sPE", bM["sPE"] + m - 1)
                    p.inc(v.tensor_tensor(PT[:, sl:sl + KB], mat[:, ch * KB:(ch + 1) * KB],
                                          mat[:, ch * KB:(ch + 1) * KB], ALU.mult),
                          "sDVE")

                @block.tensor
                def _(t, m=m, sl=sl):
                    p.wait(t, "sDVE", bM["sDVE"] + m + 1)
                    if m >= 2:
                        p.wait(t, "eDVE", bM["eDVE"] + m - 1)
                    if m == 0:
                        p.wait(t, "sDVE", masks_sDVE)
                    b2a = 2 * (m % 2)
                    t.matmul(bank(b2a), ONES[:], PT[:, sl:sl + 512],
                             start=True, stop=True)
                    p.inc(t.matmul(bank(b2a + 1), ONES[:], PT[:, sl + 512:sl + KB],
                                   start=True, stop=True), "sPE")

                @block.vector
                def _(v, m=m, run=run):
                    p.wait(v, "sPE", bM["sPE"] + m + 1)
                    b2a = 2 * (m % 2)
                    v.tensor_reduce(SM[:, 44:45], bank(b2a), AX.X, ALU.max)
                    v.tensor_reduce(SM[:, 45:46], bank(b2a + 1), AX.X, ALU.max)
                    _nops(v)
                    if m % 16 == 0:
                        v.tensor_tensor(SM[:, run:run + 1], SM[:, 44:45],
                                        SM[:, 45:46], ALU.max)
                    else:
                        v.tensor_tensor(SM[:, 44:45], SM[:, 44:45],
                                        SM[:, 45:46], ALU.max)
                        _nops(v)
                        v.tensor_tensor(SM[:, run:run + 1], SM[:, run:run + 1],
                                        SM[:, 44:45], ALU.max)
                    nv = p.inc(v.engine_nop(), "eDVE")
                    assert nv == bM["eDVE"] + m + 1

            @block.scalar
            def _(sc):
                p.wait(sc, "eDVE", bM["eDVE"] + 32)
                sc.activation(SM[:, 46:47], SM[:, 40:41], AF.Ln)
                sc.activation(SM[:, 46:47], SM[:, 46:47], AF.Exp, scale=0.5)
                sc.activation(SM[:, 47:48], SM[:, 41:42], AF.Ln)
                p.inc(sc.activation(SM[:, 47:48], SM[:, 47:48], AF.Exp, scale=0.5),
                      "sACT")

            @block.vector
            def _(v):
                p.wait(v, "sACT", bM["sACT"] + 1)
                _nops(v)
                v.tensor_tensor(SM[:, 48:49], SM[:, 46:47], SM[:, 47:48], ALU.mult)
                _nops(v)
                p.inc(v.tensor_scalar(MH[:], SM[:, 48:49], -1.0 / 128.0, None,
                                      ALU.mult), "eDVE")

        mh_eDVE = p.cnt["eDVE"]

        if on("attn"):

            epi_at = {}
            for bi in range(32):
                epi_at.setdefault(min(blk_lastwin[bi] + 2, NWIN - 1), []).append(bi)
            bcast_at = {}
            for bi in range(32):
                bcast_at.setdefault(min(blk_lastwin[bi] + 3, NWIN), []).append(bi)

            def pv_emit(t, wi):
                w = wins[wi]
                hh, bi = w["h"], w["bi"]
                if w["first"] and bi >= 2:
                    p.wait(t, "aV", bi - 1)
                    p.wait(t, "aL", bi - 1)
                p.wait(t, "aE", wi + 1)
                if win_masked(w):
                    p.wait(t, "aM", nmask_upto[wi])
                hb, db = 4 + bi % 2, 6 + bi % 2
                for ti, (kj, kr) in enumerate(w["tiles"]):
                    tv = 4 * hh + kr
                    pts = PT[:, (wi % 2) * KB + 512 * ti:(wi % 2) * KB + 512 * (ti + 1)]
                    first = w["first"] and ti == 0
                    last = w["last"] and ti == 1
                    t.matmul(bank(hb),
                             VSB[:, tv * D + 128 * kj:tv * D + 128 * (kj + 1)],
                             pts, start=first, stop=last)
                    instr = t.matmul(bank(db)[0:1, :], ONES[:, 0:1], pts,
                                     start=first, stop=last)
                p.inc(instr, "aP")

            @block.tensor
            def _(t):
                p.wait(t, "eDVE", mh_eDVE)
                p.wait(t, "sDVE", masks_sDVE)
                for wi, w in enumerate(wins):
                    hh, g, b = w["h"], w["g"], w["b"]
                    if wi >= 2:
                        p.wait(t, "aE", wi - 1)
                    sw = 2 * (wi % 2)
                    for ti, (kj, kr) in enumerate(w["tiles"]):
                        lw = KT[:, kj * KB + 512 * hh + 128 * kr:
                                kj * KB + 512 * hh + 128 * (kr + 1)]
                        for j in range(4):
                            qj = 4 * g + j
                            instr = t.matmul(bank(sw + ti)[:, 128 * j:128 * (j + 1)], lw,
                                             QT[:, qj * KB + 512 * hh + 128 * b:
                                                qj * KB + 512 * hh + 128 * (b + 1)],
                                             start=True, stop=True)
                    p.inc(instr, "aS")
                    if wi >= 1:
                        pv_emit(t, wi - 1)
                    for bi2 in bcast_at.get(wi, []):
                        db2 = 6 + bi2 % 2
                        p.wait(t, "aR", bi2 + 1)
                        p.inc(t.matmul(bank(db2), ONESF[0:1, 0:128], RCB[0:1, 0:512],
                                       start=True, stop=True), "aB")
                pv_emit(t, NWIN - 1)
                for bi2 in bcast_at.get(NWIN, []):
                    db2 = 6 + bi2 % 2
                    p.wait(t, "aR", bi2 + 1)
                    p.inc(t.matmul(bank(db2), ONESF[0:1, 0:128], RCB[0:1, 0:512],
                                   start=True, stop=True), "aB")


            @block.scalar
            def _(sc):
                for wi, w in enumerate(wins):
                    p.wait(sc, "aS", wi + 1)
                    if wi >= 2:
                        p.wait(sc, "aP", wi - 1)
                    sw = 2 * (wi % 2)
                    p.inc(sc.activation(PT[:, (wi % 2) * KB:(wi % 2 + 1) * KB],
                                        psA[:, 512 * sw:512 * sw + 1024], AF.Exp,
                                        bias=MH[:], scale=1.0 / 128.0), "aE")
                    for bi in epi_at.get(wi, []):
                        db = 6 + bi % 2
                        p.wait(sc, "aP", blk_lastwin[bi] + 1)
                        if bi >= 1:
                            p.wait(sc, "aV", bi)
                        sc.activation(LNT[0:1, :], bank(db)[0:1, :], AF.Ln)
                        p.inc(sc.activation(RCB[0:1, :], LNT[0:1, :], AF.Exp,
                                            scale=-1.0), "aR")

            evac_at = {}
            for bi in range(32):
                evac_at.setdefault(min(blk_lastwin[bi] + 3, NWIN - 1), []).append(bi)

            @block.vector
            def _(v):
                for wi, w in enumerate(wins):
                    if win_masked(w):
                        p.wait(v, "aE", wi + 1)
                        for ti, (kj, kr) in enumerate(w["tiles"]):
                            if kr == w["b"]:
                                r = min(max(kj - 4 * w["g"], 0), 4)
                                pts = PT[:, (wi % 2) * KB + 512 * ti:
                                         (wi % 2) * KB + 512 * (ti + 1)]
                                instr = v.tensor_tensor(
                                    pts, pts, MSK[:, r * 512:(r + 1) * 512], ALU.mult)
                        p.inc(instr, "aM")
                    for bi in evac_at.get(wi, []):
                        hb, db2 = 4 + bi % 2, 6 + bi % 2
                        p.wait(v, "aB", bi + 1)
                        if bi >= 2:
                            prev = bi - 2
                            p.wait(v, "dYA%d%d" % (prev // 16, prev % 2),
                                   16 * ((prev % 16) // 2 + 1))
                        p.inc(v.tensor_copy(RCB[:], bank(db2)), "aL")
                        p.inc(v.tensor_tensor(
                            AGS[:, (bi % 2) * 512:(bi % 2 + 1) * 512],
                            bank(hb), RCB[:], ALU.mult), "aV")

            @block.sync
            def _(s):
                for bi, (hh, g, b) in enumerate(att_blocks):
                    p.wait(s, "aV", bi + 1)
                    off = (4 * g) * 512 + 128 * b
                    out_ap = bass.AP(ag_in0 if hh == 0 else ag_in1, off,
                                     [[SEQ, 128], [512, 4], [1, 128]])
                    p.inc(s.dma_start(
                        out=out_ap, in_=AGS[:, (bi % 2) * 512:(bi % 2 + 1) * 512]),
                        "dYA%d%d" % (hh, bi % 2), 16)

        if on("exch"):
            @block.gpsimd
            def _(g):
                p.wait(g, "dYA00", p.cnt["dYA00"])
                p.wait(g, "dYA01", p.cnt["dYA01"])
                p.inc(g.collective_compute(
                    "AllGather", ALU.bypass, replica_groups=[list(range(NC))],
                    ins=[ag_in0.ap().opt()], outs=[ag_out0.ap().opt()]), "cCC")
                p.wait(g, "dYA10", p.cnt["dYA10"])
                p.wait(g, "dYA11", p.cnt["dYA11"])
                p.inc(g.collective_compute(
                    "AllGather", ALU.bypass, replica_groups=[list(range(NC))],
                    ins=[ag_in1.ap().opt()], outs=[ag_out1.ap().opt()]), "cCC")
                p.wait(g, "dG", 16)
                with g.register("r0") as r0:
                    p.wait(g, "cCC", 1)
                    g.reg_load(r0, CIDS[0:1, 0:1])
                    g.reg_mul(r0, r0, 64)
                    for s2 in range(8):
                        in_ap = bass.AP(ag_out0, r0, [[SEQ, 128], [512, 16], [1, 64]])
                        p.inc(g.dma_start(out=GT[:, (2 * s2) * KB:(2 * s2 + 1) * KB],
                                          in_=in_ap), "dG", 16)
                        if s2 < 7:
                            g.reg_add(r0, r0, 128 * SEQ)
                with g.register("r1") as r1:
                    p.wait(g, "cCC", 2)
                    g.reg_load(r1, CIDS[0:1, 0:1])
                    g.reg_mul(r1, r1, 64)
                    for s2 in range(8):
                        in_ap = bass.AP(ag_out1, r1, [[SEQ, 128], [512, 16], [1, 64]])
                        p.inc(g.dma_start(out=GT[:, (2 * s2 + 1) * KB:(2 * s2 + 2) * KB],
                                          in_=in_ap), "dG", 16)
                        if s2 < 7:
                            g.reg_add(r1, r1, 128 * SEQ)

        gt_dG = p.cnt["dG"]

        if on("wo"):
            bW = {k: p.cnt[k] for k in ("sPE", "eDVE", "rB")}

            @block.sync
            def _(s):
                j = 0
                for do in range(4):
                    for tt in range(8):
                        if j >= 2:
                            p.wait(s, "eDVE", bW["eDVE"] + j - 1)
                        else:
                            p.wait(s, "eDVE", ln1_eDVE)
                        p.inc(s.dma_start(
                            out=XST[:, (tt % 2) * D:(tt % 2 + 1) * D],
                            in_=xr.ap()[128 * tt:128 * (tt + 1), :]),
                            "dW%d" % (tt % 2), 16)
                        j += 1
            for j in range(32):
                do, pos = j // 8, j % 8
                if pos < 7:
                    ringB.mark("rB", bW["rB"] + do * 7 + pos + 1)
                else:
                    ringB.mark("sPE", bW["sPE"] + do + 1)

            @block.gpsimd
            def _(g):
                for t in range(32):
                    ringB.load(g, wo_g.ap()[t])

            @block.tensor
            def _(t):
                di_order = [0, 2, 4, 6, 8, 10, 12, 14, 1, 3, 5, 7, 9, 11, 13, 15]
                for do in range(4):
                    if do >= 1:
                        p.wait(t, "eDVE", bW["eDVE"] + 8 * do)
                    for ii, di in enumerate(di_order):
                        p.wait(t, "dG", 16 * (9 if ii < 8 else 17))
                        w = ringB.use(t)
                        for tt in range(8):
                            instr = t.matmul(
                                bank(tt),
                                GT[:, di * KB + 128 * tt:di * KB + 128 * (tt + 1)],
                                w, start=(ii == 0), stop=(ii == 15))
                        if ii % 2 == 1 and ii != 15:
                            p.inc(instr, "rB")
                    p.inc(instr, "sPE")

            @block.vector
            def _(v):
                j = 0
                for do in range(4):
                    p.wait(v, "sPE", bW["sPE"] + do + 1)
                    for tt in range(8):
                        p.wait(v, "dW%d" % (tt % 2), 16 * (4 * do + tt // 2 + 1))
                        p.inc(v.tensor_tensor(
                            X1[:, tt * D + 512 * do:tt * D + 512 * (do + 1)],
                            bank(tt),
                            XST[:, (tt % 2) * D + 512 * do:(tt % 2) * D + 512 * (do + 1)],
                            ALU.add), "eDVE")
                        j += 1

        wo_eDVE = p.cnt["eDVE"]

        if on("ln2"):
            bL2 = {k: p.cnt[k] for k in ("eDVE", "sACT", "sDVE", "sPE")}
            e0, s0 = bL2["eDVE"], bL2["sPE"]
            for k in range(8):
                x1t = X1[:, k * D:(k + 1) * D]
                if k >= 1:
                    emit_ln_copies(k - 1, 128 * (k - 1),
                                   dict(sPE0=s0 + 4 * (k - 1),
                                        copies_eDVE=e0 + 5 * (k - 1) + 1))

                @block.sync
                def _(s, k=k, x1t=x1t):
                    p.wait(s, "eDVE", wo_eDVE - 8 + k + 1)
                    p.inc(s.dma_start(out=x1s.ap()[128 * k:128 * (k + 1), :],
                                      in_=x1t), "dSP", 16)

                emit_ln_tile(
                    k, x1t, 128 * k,
                    dict(src_waits=[("eDVE", wo_eDVE - 8 + k + 1)],
                         sq_guard=(e0 + 5 * (k - 2) + 1) if k >= 2 else 0,
                         sACT0=bL2["sACT"] + 2 * k,
                         sDVE_val=bL2["sDVE"] + k + 1,
                         norm_eDVE=e0 + 5 * k + 1,
                         pe_round_waits=[(e0 + 5 * k - 1) if k >= 1 else 0,
                                         (e0 + 5 * k) if k >= 1 else 0,
                                         e0 + 5 * k + 2,
                                         e0 + 5 * k + 3],
                         sPE0=s0 + 4 * k))
            emit_ln_copies(7, 128 * 7, dict(sPE0=s0 + 28, copies_eDVE=e0 + 36))

        ln2_eDVE = p.cnt["eDVE"]
        spill_dSP = p.cnt["dSP"]

        def ffn_12(asrc, dst, bias_sb, nk, wsrc, ngroups):
            b0 = {k: p.cnt[k] for k in ("sPE", "eACT", "rA")}
            gper = nk // 8
            for j in range(ngroups):
                m, pos = j // gper, j % gper
                if pos < gper - 1:
                    ringA.mark("rA", b0["rA"] + m * (gper - 1) + pos + 1)
                else:
                    ringA.mark("sPE", b0["sPE"] + m + 1)

            @block.gpsimd
            def _(g):
                for t in range(ngroups):
                    ringA.load(g, wsrc.ap()[t])

            @block.tensor
            def _(t):
                p.wait(t, "eDVE", ln2_eDVE)
                p.wait(t, "dSP", spill_dSP)
                for m in range(32):
                    pr2 = m % 2
                    p.wait(t, "eACT", b0["eACT"] + max(0, m - 1))
                    for kc in range(nk):
                        w = ringA.use(t)
                        t.matmul(bank(0 + 2 * pr2), w, asrc[:, kc * KB:kc * KB + 512],
                                 start=(kc == 0), stop=(kc == nk - 1))
                        instr = t.matmul(bank(1 + 2 * pr2), w,
                                         asrc[:, kc * KB + 512:(kc + 1) * KB],
                                         start=(kc == 0), stop=(kc == nk - 1))
                        if kc % 8 == 7 and kc != nk - 1:
                            p.inc(instr, "rA")
                    p.inc(instr, "sPE")

            @block.scalar
            def _(sc):
                p.wait(sc, "dSP", spill_dSP)
                p.wait(sc, "dC", 32)
                for m in range(32):
                    pr2 = m % 2
                    p.wait(sc, "sPE", b0["sPE"] + m + 1)
                    sc.activation(dst[:, m * KB:m * KB + 512], bank(0 + 2 * pr2),
                                  AF.Relu, bias=bias_sb[:, m:m + 1])
                    p.inc(sc.activation(dst[:, m * KB + 512:(m + 1) * KB],
                                        bank(1 + 2 * pr2), AF.Relu,
                                        bias=bias_sb[:, m:m + 1]), "eACT")

        if on("ffn1"):
            ffn_12(HT, Z1, B1S, 16, w1_g, 64)
        if on("ffn2"):
            ffn_12(Z1, Z2, B2S, 32, w2_g, 128)

        if on("ffn3"):
            bF3 = {k: p.cnt[k] for k in ("sPE", "eDVE", "dYY", "eACT")}
            nB = [0, 0]
            nXf = [nX[0], nX[1]]
            bF3["rB"] = p.cnt["rB"]
            for j in range(64):
                do, pos = j // 16, j % 16
                if pos < 15:
                    ringB.mark("rB", bF3["rB"] + do * 15 + pos + 1)
                else:
                    ringB.mark("sPE", bF3["sPE"] + do + 1)

            @block.gpsimd
            def _(g):
                for t in range(64):
                    ringB.load(g, w3_g.ap()[t])

            X1R4 = [X1R[:, 0:512], X1R[:, 512:1024], B3S[:, 0:512], B3S[:, 512:1024],
                    XST[:, 1536:2048], X1T0[:], X1T1[:], X1T2[:]]
            B3F = XST[:, 2048:4096]
            FT3 = XST[:, 1024:1536]

            @block.sync
            def _(s):
                p.wait(s, "dSP", spill_dSP)
                p.inc(s.dma_start(out=B3F, in_=b3d.ap()), "dB0", 16)
                q = 0
                for do in range(4):
                    for tt in range(8):
                        if q >= 8:
                            p.wait(s, "eDVE", bF3["eDVE"] + q - 7)
                        p.inc(s.dma_start(
                            out=X1R4[q % 8],
                            in_=x1s.ap()[128 * tt:128 * (tt + 1),
                                         512 * do:512 * (do + 1)]),
                            "dR%d" % (q % 8), 16)
                        if q >= 1:
                            qs = q - 1
                            p.wait(s, "eDVE", bF3["eDVE"] + qs + 1)
                            p.inc(s.dma_start(
                                out=y.ap()[128 * (qs % 8):128 * (qs % 8 + 1),
                                           512 * (qs // 8):512 * (qs // 8 + 1)],
                                in_=YST[:, (qs % 2) * 512:(qs % 2 + 1) * 512]),
                                "dY%d" % (qs % 2), 16)
                        q += 1
                p.wait(s, "eDVE", bF3["eDVE"] + 32)
                p.inc(s.dma_start(
                    out=y.ap()[128 * 7:128 * 8, 512 * 3:512 * 4],
                    in_=YST[:, 512:1024]), "dY1", 16)

            @block.tensor
            def _(t):
                p.wait(t, "eACT", p.cnt["eACT"])
                for do in range(4):
                    if do >= 1:
                        p.wait(t, "eDVE", bF3["eDVE"] + 8 * do)
                    for kc in range(32):
                        w = ringB.use(t)
                        for tt in range(8):
                            instr = t.matmul(
                                bank(tt),
                                Z2[:, kc * KB + 128 * tt:kc * KB + 128 * (tt + 1)],
                                w, start=(kc == 0), stop=(kc == 31))
                        if kc % 2 == 1 and kc != 31:
                            p.inc(instr, "rB")
                    p.inc(instr, "sPE")

            @block.vector
            def _(v):
                q = 0
                p.wait(v, "dB0", 16)
                for do in range(4):
                    p.wait(v, "sPE", bF3["sPE"] + do + 1)
                    for tt in range(8):
                        p.wait(v, "dR%d" % (q % 8), 16 * (q // 8 + 1))
                        if q >= 2:
                            p.wait(v, "dY%d" % (q % 2), 16 * ((q - 2) // 2 + 1))
                        v.tensor_tensor(FT3, bank(tt), X1R4[q % 8], ALU.add)
                        p.inc(v.tensor_tensor(
                            YST[:, (q % 2) * 512:(q % 2 + 1) * 512], FT3,
                            B3F[:, 512 * do:512 * (do + 1)], ALU.add),
                            "eDVE")
                        q += 1

        sb_map = {"hT": HT, "QT": QT, "KT": KT, "V": VSB, "GT": GT, "h2T": HT,
                  "x1": X1, "z1T": Z1, "z2T": Z2, "mh": MH}

        @block.gpsimd
        def _(g):
            for nm in ["eDVE", "eACT", "sACT", "sPE", "aV", "aP", "aL", "dG",
                       "dSP", "dYY", "dY0", "dY1",
                       "dYA00", "dYA01", "dYA10", "dYA11"]:
                p.wait(g, nm, p.cnt[nm])
            for k in debug:
                if k == "agin":
                    p.inc(g.dma_start(out=GT[:, 0:SEQ], in_=ag_in0.ap()), "dYY", 16)
                    p.inc(g.dma_start(out=GT[:, SEQ:2 * SEQ],
                                      in_=ag_in1.ap()), "dYY", 16)
                    p.wait(g, "dYY", p.cnt["dYY"])
                    instr = g.dma_start(out=dbg[k].ap(), in_=GT[:, 0:2 * SEQ])
                else:
                    instr = g.dma_start(out=dbg[k].ap(), in_=sb_map[k][:])
                p.inc(instr, "dYY", 16)
            p.wait(g, "dYY", p.cnt["dYY"])

        @block.sync
        def _(s):
            p.wait(s, "dYY", p.cnt["dYY"])
            p.wait(s, "dY0", p.cnt["dY0"])
            p.wait(s, "dY1", p.cnt["dY1"])

    return nc


_CACHE = {}


def _perm(c):
    t = np.arange(1024)
    return 16 * (64 * c + (t % 64)) + (t // 64)


WO_PERM = [0, 2, 4, 6, 8, 10, 12, 14, 1, 3, 5, 7, 9, 11, 13, 15]


def _pretile(w, kchunks, mchunks, mwidth, group, kperm=None):
    wt = np.ascontiguousarray(np.asarray(w, np.float32).T)
    a = wt.reshape(kchunks, 128, mchunks, mwidth).transpose(2, 0, 1, 3)
    tiles = a.reshape(mchunks * kchunks, 128, mwidth)
    if kperm is not None:
        tiles = tiles.reshape(mchunks, kchunks, 128, mwidth)[:, kperm]
        tiles = tiles.reshape(mchunks * kchunks, 128, mwidth)
    gshape = (tiles.shape[0] // group, group, 128, mwidth)
    gg = tiles.reshape(gshape).transpose(0, 2, 1, 3)
    return np.ascontiguousarray(gg.reshape(gshape[0], 128, group * mwidth))


def kernel(L0_attn_x, L0_attn_Wq, L0_attn_Wk, L0_attn_Wv, L0_attn_Wo,
           L0_ffn_W1, L0_ffn_b1, L0_ffn_W2, L0_ffn_b2, L0_ffn_W3, L0_ffn_b3):
    x = np.asarray(L0_attn_x, dtype=np.float32)[0]
    if "nc" not in _CACHE:
        _CACHE["nc"] = build_program()
    nc = _CACHE["nc"]
    wq = _pretile(L0_attn_Wq, 16, 16, 128, 8)
    wk = _pretile(L0_attn_Wk, 16, 16, 128, 8)
    wv = _pretile(L0_attn_Wv, 16, 4, 512, 2)
    wo = _pretile(L0_attn_Wo, 16, 4, 512, 2, WO_PERM)
    w1 = _pretile(L0_ffn_W1, 16, 32, 128, 8)
    w2 = _pretile(L0_ffn_W2, 32, 32, 128, 8)
    w3 = _pretile(L0_ffn_W3, 32, 4, 512, 2)
    b1 = np.ascontiguousarray(np.asarray(L0_ffn_b1, np.float32).reshape(32, 128).T)
    b2 = np.ascontiguousarray(np.asarray(L0_ffn_b2, np.float32).reshape(32, 128).T)
    b3 = np.ascontiguousarray(
        np.broadcast_to(np.asarray(L0_ffn_b3, np.float32), (128, D)))
    in_maps, perms = [], []
    for c in range(NC):
        pc = _perm(c)
        perms.append(pc)
        in_maps.append(dict(
            xq=np.ascontiguousarray(x[1024 * c:1024 * (c + 1)]),
            xr=np.ascontiguousarray(x[pc]),
            cid=np.array([[c]], dtype=np.int32),
            wq_g=wq, wk_g=wk, wv_g=wv, wo_g=wo, w1_g=w1, w2_g=w2, w3_g=w3,
            b1d=b1, b2d=b2, b3d=b3))
    res = run_bass_kernel_spmd(nc, in_maps, core_ids=list(range(NC)))
    out = np.empty((SEQ, D), dtype=np.float32)
    for c in range(NC):
        out[perms[c]] = res.results[c]["y"]
    return out.reshape(1, SEQ, D)


# version 09
# speedup vs baseline: 1.0239x; 1.0239x over previous
import contextlib
import numpy as np
import ml_dtypes
import concourse.bass as bass
import concourse.mybir as mybir
from concourse.bass_utils import run_bass_kernel_spmd

F32 = mybir.dt.float32
BF16 = mybir.dt.bfloat16
I32 = mybir.dt.int32
AF = mybir.ActivationFunctionType
ALU = mybir.AluOpType
AX = mybir.AxisListType

D = 2048
SEQ = 8192
NC = 8
KB = 1024
PHASES = ("ln1", "qkv", "mh", "attn", "exch", "wo", "ln2", "ffn1", "ffn2", "ffn3")


class P:
    def __init__(self, nc, st):
        self.nc, self.st = nc, st
        self.sem, self.cnt = {}, {}

    def mksem(self, name):
        self.sem[name] = self.st.enter_context(self.nc.semaphore(name))
        self.cnt[name] = 0

    def inc(self, instr, name, n=1):
        instr.then_inc(self.sem[name], n)
        self.cnt[name] += n
        return self.cnt[name]

    def wait(self, eng, name, val):
        if val > 0:
            eng.wait_ge(self.sem[name], val)


def _nops(eng, n=6):
    for _ in range(n):
        eng.engine_nop()


class Ring:

    def __init__(self, p, name, w, group, sb):
        self.p, self.name, self.w, self.group, self.sb = p, name, w, group, sb
        self.gw = w * group
        self.nload = 0
        self.ncons = 0
        self.marks = []
        for s in range(3):
            p.mksem(f"l{name}{s}")

    def mark(self, sem, val):
        self.marks.append((sem, val))

    def load(self, g, dram_ap):
        m = self.nload
        if m >= 3:
            sem, val = self.marks[m - 3]
            self.p.wait(g, sem, val)
        s = m % 3
        slot = self.sb[:, s * self.gw:(s + 1) * self.gw]
        self.p.inc(g.dma_start(out=slot, in_=dram_ap), f"l{self.name}{s}", 16)
        self.nload += 1

    def use(self, eng):
        k = self.ncons
        m, i = k // self.group, k % self.group
        if i == 0:
            self.p.wait(eng, f"l{self.name}{m % 3}", 16 * (m // 3 + 1))
        base = (m % 3) * self.gw + i * self.w
        self.ncons += 1
        return self.sb[:, base:base + self.w]


def build_program(upto="ffn3", debug=()):
    sub = None
    if upto in ("qkva", "qkvb"):
        sub, upto = upto, "qkv"
    nc = bass.Bass("TRN2", target_bir_lowering=False, debug=False,
                   detect_race_conditions=False)
    nup = PHASES.index(upto)

    def on(ph):
        return PHASES.index(ph) <= nup

    xq = nc.dram_tensor("xq", [1024, D], F32, kind="ExternalInput")
    xr = nc.dram_tensor("xr", [1024, D], F32, kind="ExternalInput")
    cid = nc.dram_tensor("cid", [1, 1], I32, kind="ExternalInput")
    wq_g = nc.dram_tensor("wq_g", [32, 128, 1024], F32, kind="ExternalInput")
    wk_g = nc.dram_tensor("wk_g", [32, 128, 1024], F32, kind="ExternalInput")
    wv_g = nc.dram_tensor("wv_g", [32, 128, 1024], F32, kind="ExternalInput")
    wo_g = nc.dram_tensor("wo_g", [32, 128, 1024], F32, kind="ExternalInput")
    w1_g = nc.dram_tensor("w1_g", [64, 128, 1024], F32, kind="ExternalInput")
    w2_g = nc.dram_tensor("w2_g", [128, 128, 1024], F32, kind="ExternalInput")
    w3_g = nc.dram_tensor("w3_g", [64, 128, 1024], F32, kind="ExternalInput")
    b1d = nc.dram_tensor("b1d", [128, 32], F32, kind="ExternalInput")
    b2d = nc.dram_tensor("b2d", [128, 32], F32, kind="ExternalInput")
    b3d = nc.dram_tensor("b3d", [128, D], F32, kind="ExternalInput")
    y = nc.dram_tensor("y", [1024, D], F32, kind="ExternalOutput")
    ag_in0 = nc.dram_tensor("ag_in0", [128, SEQ], BF16)
    ag_in1 = nc.dram_tensor("ag_in1", [128, SEQ], BF16)
    ag_out0 = nc.dram_tensor("ag_out0", [NC * 128, SEQ], BF16)
    ag_out1 = nc.dram_tensor("ag_out1", [NC * 128, SEQ], BF16)
    x1s = nc.dram_tensor("x1s", [1024, D], F32)
    dbg_shapes = {"hT": [128, 16 * KB], "QT": [128, 16 * KB],
                  "KT": [128, 16 * KB], "V": [128, 8 * D],
                  "GT": [128, 16 * KB], "h2T": [128, 16 * KB],
                  "x1": [128, 16 * KB], "z1T": [128, 32 * KB],
                  "z2T": [128, 32 * KB], "agin": [128, 2 * SEQ],
                  "mh": [128, 1]}
    dbg = {k: nc.dram_tensor("dbg_" + k, dbg_shapes[k], F32, kind="ExternalOutput")
           for k in debug}

    st = contextlib.ExitStack()
    with st:
        SB_BASE = 20480

        def place(name, shape, dtype, off):
            return nc.alloc_sbuf_tensor_at(name, shape, dtype, offset=SB_BASE + off)

        HT = place("HT", [128, 16 * KB], BF16, 0)
        QT = place("QT", [128, 16 * KB], BF16, 32768)
        KT = place("KT", [128, 16 * KB], BF16, 65536)
        X1 = place("X1", [128, 16 * KB], F32, 32768)
        Z1 = place("Z1", [128, 32 * KB], BF16, 32768)
        VSB = place("VSB", [128, 8 * D], BF16, 98304)
        Z2 = place("Z2", [128, 32 * KB], BF16, 98304)
        GT = place("GT", [128, 16 * KB], BF16, 131072)
        MSK = place("MSK", [128, 5 * 512], BF16, 131072)
        HSB = place("HSB", [128, D], BF16, 136192)
        HSB2 = place("HSB2", [128, D], BF16, 140288)
        PT = place("PT", [128, 2 * KB], BF16, 144384)
        AGS = place("AGS", [128, 2 * 512], BF16, 148480)
        RCB = place("RCB", [128, 512], F32, 150528)
        LNT = place("LNT", [128, 512], F32, 152576)
        XST = place("XST", [128, 2 * D], F32, 163840)
        X1R = place("X1R", [128, 2 * 512], F32, 180224)
        B3S = place("B3S", [128, 2 * 512], F32, 184320)
        RA = place("RA", [128, 3 * 8 * 128], BF16, 188416)
        RB = place("RB", [128, 3 * 2 * 512], BF16, 194560)
        SM = place("SM", [128, 64], F32, 200704)
        SMB = place("SMB", [128, 16], BF16, 200960)
        IDNB = place("IDNB", [128, 128], BF16, 201024)
        ONES = place("ONES", [128, 128], BF16, 201536)
        B1S = place("B1S", [128, 32], F32, 201792)
        B2S = place("B2S", [128, 32], F32, 201920)
        MH = place("MH", [128, 1], F32, 202048)
        CIDS = place("CIDS", [1, 1], I32, 202080)
        ONESF = place("ONESF", [128, 128], F32, 202112)
        X1T0 = place("X1T0", [128, 512], F32, 202624)
        X1T1 = place("X1T1", [128, 512], F32, 204672)
        X1T2 = place("X1T2", [128, 512], F32, 206720)
        YST = XST

        psA = st.enter_context(nc.psum_tensor("psA", [128, 4096], F32))

        def bank(b):
            return psA[:, 512 * b:512 * (b + 1)]

        def bankb(b):
            return psA[:, 512 * b:512 * (b + 1)].bitcast(BF16)

        p = P(nc, st)
        for s in ["dX0", "dX1", "dB0", "dB1", "dC", "dG", "dSP", "dYY",
                  "dYA00", "dYA01", "dYA10", "dYA11",
                  "sPE", "sACT", "sDVE", "eDVE", "eACT",
                  "aS", "aE", "aM", "aP", "aL", "aR", "aV", "aB", "cCC", "rA", "rB",
                  "dY0", "dY1", "dR0", "dR1", "dR2", "dR3",
                  "dR4", "dR5", "dR6", "dR7", "dW0", "dW1"]:
            p.mksem(s)
        ringA = Ring(p, "A", 128, 8, RA)
        ringB = Ring(p, "B", 512, 2, RB)

        block = st.enter_context(nc.Block())

        att_blocks = [(h, g, b) for h in range(2) for g in range(4) for b in range(4)]
        wins = []
        for bi, (h, g, b) in enumerate(att_blocks):
            tl = [(kj, kr) for kj in range(16) for kr in range(b + 1)]
            for w in range(len(tl) // 2):
                wins.append(dict(bi=bi, h=h, g=g, b=b,
                                 tiles=(tl[2 * w], tl[2 * w + 1]),
                                 first=(w == 0), last=(w == len(tl) // 2 - 1)))
        NWIN = len(wins)
        blk_lastwin = {}
        for wi, w in enumerate(wins):
            blk_lastwin[w["bi"]] = wi

        def win_masked(w):
            return any(kr == w["b"] for kj, kr in w["tiles"])

        nmask_upto = []
        c_ = 0
        for w in wins:
            c_ += 1 if win_masked(w) else 0
            nmask_upto.append(c_)

        @block.gpsimd
        def _(g):
            g.memset(ONES[:], 1.0)
            g.memset(ONESF[:], 1.0)
            g.memset(SMB[:], 1.0)
            g.memset(SM[:], 0.0)
            g.iota(IDNB[:], [[1, 128]], base=0, channel_multiplier=-1,
                   allow_small_or_imprecise_dtypes=True)
            instr = None
            for r in range(5):
                for j in range(4):
                    instr = g.iota(MSK[:, r * 512 + j * 128:r * 512 + (j + 1) * 128],
                                   [[1, 128]], base=(0 if j >= r else -1),
                                   channel_multiplier=-1,
                                   allow_small_or_imprecise_dtypes=True)
            p.inc(instr, "sDVE")
            p.inc(g.dma_start(out=CIDS[:], in_=cid.ap()), "dG", 16)

        setup_sDVE = p.cnt["sDVE"]

        @block.vector
        def _(v):
            p.wait(v, "sDVE", setup_sDVE)
            for r in range(5):
                v.tensor_scalar(MSK[:, r * 512:(r + 1) * 512],
                                MSK[:, r * 512:(r + 1) * 512], 0.0, None, ALU.is_ge)
            p.inc(v.tensor_scalar(IDNB[:], IDNB[:], 0.0, None, ALU.is_equal), "sDVE")

        masks_sDVE = p.cnt["sDVE"]

        def emit_ln_tile(k, src_ap, ht_tokbase, W):
            HSBk = HSB if k % 2 == 0 else HSB2
            """Emit LN + transpose of one 128-token tile into HT (bf16).

            W (absolute semaphore values):
              src_waits: [(sem, val)] before touching src (scalar AND vector)
              sq_guard:  eDVE value guarding SM parity slots (0 = skip)
              sACT0:     sACT count before this tile's two incs
              sDVE_val:  sDVE count after this tile's var inc
              norm_eDVE: eDVE count after this tile's normalize inc
              pe_round_waits: 4 absolute eDVE values (0 = skip) per PE round
              sPE0:      sPE count before this tile's 4 round incs
            Emits: sACT +2, sDVE +1, eDVE +5 (normalize + 4 copy rounds),
            sPE +4.
            """
            pr = (k % 2) * 16

            @block.scalar
            def _(sc):
                for sem, val in W["src_waits"]:
                    p.wait(sc, sem, val)
                if W["sq_guard"]:
                    p.wait(sc, "eDVE", W["sq_guard"])
                p.wait(sc, "sPE", max(W["sPE0"] - 4, 0))
                nv = p.inc(sc.activation(HSBk[:], src_ap, AF.Square,
                                         accum_out=SM[:, pr + 1:pr + 2]), "sACT")
                assert nv == W["sACT0"] + 1

            @block.vector
            def _(v):
                for sem, val in W["src_waits"]:
                    p.wait(v, sem, val)
                if W["sq_guard"]:
                    p.wait(v, "eDVE", W["sq_guard"])
                v.tensor_reduce(SM[:, pr + 0:pr + 1], src_ap, AX.X, ALU.add)
                p.wait(v, "sACT", W["sACT0"] + 1)
                _nops(v)
                v.tensor_scalar(SM[:, pr + 3:pr + 4], SM[:, pr + 0:pr + 1],
                                -1.0 / D, None, ALU.mult)
                _nops(v)
                v.tensor_tensor(SM[:, pr + 4:pr + 5], SM[:, pr + 3:pr + 4],
                                SM[:, pr + 3:pr + 4], ALU.mult)
                v.tensor_scalar(SM[:, pr + 5:pr + 6], SM[:, pr + 1:pr + 2],
                                1.0 / D, None, ALU.mult)
                _nops(v)
                v.tensor_tensor(SM[:, pr + 6:pr + 7], SM[:, pr + 5:pr + 6],
                                SM[:, pr + 4:pr + 5], ALU.subtract)
                _nops(v)
                nv = p.inc(v.tensor_scalar(SM[:, pr + 9:pr + 10],
                                           SM[:, pr + 6:pr + 7], 1e-5, None,
                                           ALU.add), "sDVE")
                assert nv == W["sDVE_val"]

            @block.scalar
            def _(sc):
                p.wait(sc, "sDVE", W["sDVE_val"])
                sc.activation(SM[:, pr + 7:pr + 8], SM[:, pr + 9:pr + 10], AF.Ln)
                nv = p.inc(sc.activation(SM[:, pr + 8:pr + 9], SM[:, pr + 7:pr + 8],
                                         AF.Exp, scale=-0.5), "sACT")
                assert nv == W["sACT0"] + 2

            @block.vector
            def _(v):
                p.wait(v, "sACT", W["sACT0"] + 2)
                _nops(v)
                nv = p.inc(v.tensor_scalar(HSBk[:], src_ap, SM[:, pr + 3:pr + 4],
                                           SM[:, pr + 8:pr + 9], ALU.add,
                                           ALU.mult), "eDVE")
                assert nv == W["norm_eDVE"]

            @block.tensor
            def _(t):
                p.wait(t, "sDVE", masks_sDVE)
                p.wait(t, "eDVE", W["norm_eDVE"])
                for r in range(4):
                    if W["pe_round_waits"][r]:
                        p.wait(t, "eDVE", W["pe_round_waits"][r])
                    for s4 in range(4):
                        jj = 4 * r + s4
                        b = 4 * (r % 2) + s4
                        instr = t.transpose(bankb(b)[:, 0:128],
                                            HSBk[:, 128 * jj:128 * (jj + 1)], IDNB[:])
                    nv = p.inc(instr, "sPE")
                    assert nv == W["sPE0"] + r + 1


        def emit_ln_copies(k, ht_tokbase, W):
            @block.vector
            def _(v):
                for r in range(4):
                    p.wait(v, "sPE", W["sPE0"] + r + 1)
                    for s4 in range(4):
                        jj = 4 * r + s4
                        b = 4 * (r % 2) + s4
                        instr = v.tensor_copy(
                            HT[:, jj * KB + ht_tokbase:jj * KB + ht_tokbase + 128],
                            bankb(b)[:, 0:128])
                    nv = p.inc(instr, "eDVE")
                    assert nv == W["copies_eDVE"] + r + 1

        b_ln1 = {k: p.cnt[k] for k in ("eDVE", "sACT", "sDVE", "sPE")}
        nX = [0, 0]

        @block.sync
        def _(s):
            p.inc(s.dma_start(out=B1S[:], in_=b1d.ap()), "dC", 16)
            p.inc(s.dma_start(out=B2S[:], in_=b2d.ap()), "dC", 16)
            for k in range(8):
                if k >= 2:
                    p.wait(s, "eDVE", b_ln1["eDVE"] + 5 * (k - 2) + 1)
                nX[k % 2] += 1
                p.inc(s.dma_start(out=XST[:, (k % 2) * D:(k % 2 + 1) * D],
                                  in_=xq.ap()[128 * k:128 * (k + 1), :]),
                      "dX%d" % (k % 2), 16)

        if on("ln1"):
            e0, s0 = b_ln1["eDVE"], b_ln1["sPE"]
            for k in range(8):
                if k >= 1:
                    emit_ln_copies(k - 1, 128 * (k - 1),
                                   dict(sPE0=s0 + 4 * (k - 1),
                                        copies_eDVE=e0 + 5 * (k - 1) + 1))
                emit_ln_tile(
                    k, XST[:, (k % 2) * D:(k % 2 + 1) * D], 128 * k,
                    dict(src_waits=[("dX%d" % (k % 2), 16 * (k // 2 + 1))],
                         sq_guard=(e0 + 5 * (k - 2) + 1) if k >= 2 else 0,
                         sACT0=b_ln1["sACT"] + 2 * k,
                         sDVE_val=b_ln1["sDVE"] + k + 1,
                         norm_eDVE=e0 + 5 * k + 1,
                         pe_round_waits=[(e0 + 5 * k - 1) if k >= 1 else 0,
                                         (e0 + 5 * k) if k >= 1 else 0,
                                         e0 + 5 * k + 2,
                                         e0 + 5 * k + 3],
                         sPE0=s0 + 4 * k))
            emit_ln_copies(7, 128 * 7, dict(sPE0=s0 + 28, copies_eDVE=e0 + 36))
        ln1_eDVE = p.cnt["eDVE"]

        if on("qkv"):
            def proj_qk(dst, wsrc):
                b0 = {k: p.cnt[k] for k in ("sPE", "eDVE", "rA")}
                for j in range(32):
                    if j % 2 == 0:
                        ringA.mark("rA", b0["rA"] + j // 2 + 1)
                    else:
                        ringA.mark("sPE", b0["sPE"] + j // 2 + 1)

                @block.gpsimd
                def _(g):
                    for t in range(32):
                        ringA.load(g, wsrc.ap()[t])

                @block.tensor
                def _(t):
                    p.wait(t, "eDVE", ln1_eDVE)
                    for dq in range(16):
                        pr2 = dq % 2
                        if dq >= 2:
                            p.wait(t, "eDVE", b0["eDVE"] + dq - 1)
                        for kc in range(16):
                            w = ringA.use(t)
                            t.matmul(bank(4 + 2 * pr2), w, HT[:, kc * KB:kc * KB + 512],
                                     start=(kc == 0), stop=(kc == 15))
                            instr = t.matmul(bank(5 + 2 * pr2), w,
                                             HT[:, kc * KB + 512:(kc + 1) * KB],
                                             start=(kc == 0), stop=(kc == 15))
                            if kc == 7:
                                p.inc(instr, "rA")
                        p.inc(instr, "sPE")

                @block.vector
                def _(v):
                    for dq in range(16):
                        pr2 = dq % 2
                        p.wait(v, "sPE", b0["sPE"] + dq + 1)
                        v.tensor_copy(dst[:, dq * KB:dq * KB + 512], bank(4 + 2 * pr2))
                        p.inc(v.tensor_copy(dst[:, dq * KB + 512:(dq + 1) * KB],
                                            bank(5 + 2 * pr2)), "eDVE")

            proj_qk(QT, wq_g)
            if sub != "qkva":
                proj_qk(KT, wk_g)
            qk_eDVE = p.cnt["eDVE"]
            bV = {k: p.cnt[k] for k in ("sPE", "eDVE", "rB")}
            if sub not in ("qkva", "qkvb"):
                for j in range(32):
                    dv, pos = j // 8, j % 8
                    if pos < 7:
                        ringB.mark("rB", bV["rB"] + dv * 7 + pos + 1)
                    else:
                        ringB.mark("sPE", bV["sPE"] + dv + 1)

                @block.gpsimd
                def _(g):
                    for t in range(32):
                        ringB.load(g, wv_g.ap()[t])

                @block.tensor
                def _(t):
                    p.wait(t, "eDVE", qk_eDVE)
                    for dv in range(4):
                        if dv >= 1:
                            p.wait(t, "eDVE", bV["eDVE"] + dv)
                        for kc in range(16):
                            w = ringB.use(t)
                            for tv in range(8):
                                instr = t.matmul(
                                    bank(tv),
                                    HT[:, kc * KB + 128 * tv:kc * KB + 128 * (tv + 1)],
                                    w, start=(kc == 0), stop=(kc == 15))
                            if kc % 2 == 1 and kc != 15:
                                p.inc(instr, "rB")
                        p.inc(instr, "sPE")

                @block.vector
                def _(v):
                    for dv in range(4):
                        p.wait(v, "sPE", bV["sPE"] + dv + 1)
                        for tv in range(8):
                            instr = v.tensor_copy(
                                VSB[:, tv * D + dv * 512:tv * D + (dv + 1) * 512],
                                bank(tv))
                        p.inc(instr, "eDVE")

        v_eDVE = p.cnt["eDVE"]

        if on("mh"):
            bM = {k: p.cnt[k] for k in ("sDVE", "sPE", "eDVE", "sACT")}

            for m in range(32):
                mat = QT if m < 16 else KT
                ch = m % 16
                run = 40 if m < 16 else 41
                sl = (m % 2) * KB

                @block.vector
                def _(v, m=m, mat=mat, ch=ch, sl=sl):
                    if m == 0:
                        p.wait(v, "eDVE", v_eDVE)
                    if m >= 2:
                        p.wait(v, "sPE", bM["sPE"] + m - 1)
                    p.inc(v.tensor_tensor(PT[:, sl:sl + KB], mat[:, ch * KB:(ch + 1) * KB],
                                          mat[:, ch * KB:(ch + 1) * KB], ALU.mult),
                          "sDVE")

                @block.tensor
                def _(t, m=m, sl=sl):
                    p.wait(t, "sDVE", bM["sDVE"] + m + 1)
                    if m >= 2:
                        p.wait(t, "eDVE", bM["eDVE"] + m - 1)
                    if m == 0:
                        p.wait(t, "sDVE", masks_sDVE)
                    b2a = 2 * (m % 2)
                    t.matmul(bank(b2a), ONES[:], PT[:, sl:sl + 512],
                             start=True, stop=True)
                    p.inc(t.matmul(bank(b2a + 1), ONES[:], PT[:, sl + 512:sl + KB],
                                   start=True, stop=True), "sPE")

                @block.vector
                def _(v, m=m, run=run):
                    p.wait(v, "sPE", bM["sPE"] + m + 1)
                    b2a = 2 * (m % 2)
                    v.tensor_reduce(SM[:, 44:45], bank(b2a), AX.X, ALU.max)
                    v.tensor_reduce(SM[:, 45:46], bank(b2a + 1), AX.X, ALU.max)
                    _nops(v)
                    if m % 16 == 0:
                        v.tensor_tensor(SM[:, run:run + 1], SM[:, 44:45],
                                        SM[:, 45:46], ALU.max)
                    else:
                        v.tensor_tensor(SM[:, 44:45], SM[:, 44:45],
                                        SM[:, 45:46], ALU.max)
                        _nops(v)
                        v.tensor_tensor(SM[:, run:run + 1], SM[:, run:run + 1],
                                        SM[:, 44:45], ALU.max)
                    nv = p.inc(v.engine_nop(), "eDVE")
                    assert nv == bM["eDVE"] + m + 1

            @block.scalar
            def _(sc):
                p.wait(sc, "eDVE", bM["eDVE"] + 32)
                sc.activation(SM[:, 46:47], SM[:, 40:41], AF.Ln)
                sc.activation(SM[:, 46:47], SM[:, 46:47], AF.Exp, scale=0.5)
                sc.activation(SM[:, 47:48], SM[:, 41:42], AF.Ln)
                p.inc(sc.activation(SM[:, 47:48], SM[:, 47:48], AF.Exp, scale=0.5),
                      "sACT")

            @block.vector
            def _(v):
                p.wait(v, "sACT", bM["sACT"] + 1)
                _nops(v)
                v.tensor_tensor(SM[:, 48:49], SM[:, 46:47], SM[:, 47:48], ALU.mult)
                _nops(v)
                p.inc(v.tensor_scalar(MH[:], SM[:, 48:49], -1.0 / 128.0, None,
                                      ALU.mult), "eDVE")

        mh_eDVE = p.cnt["eDVE"]

        if on("attn"):

            epi_at = {}
            for bi in range(32):
                epi_at.setdefault(min(blk_lastwin[bi] + 2, NWIN - 1), []).append(bi)
            bcast_at = {}
            for bi in range(32):
                bcast_at.setdefault(min(blk_lastwin[bi] + 3, NWIN), []).append(bi)

            def pv_emit(t, wi):
                w = wins[wi]
                hh, bi = w["h"], w["bi"]
                if w["first"] and bi >= 2:
                    p.wait(t, "aV", bi - 1)
                    p.wait(t, "aL", bi - 1)
                p.wait(t, "aE", wi + 1)
                if win_masked(w):
                    p.wait(t, "aM", nmask_upto[wi])
                hb, db = 4 + bi % 2, 6 + bi % 2
                for ti, (kj, kr) in enumerate(w["tiles"]):
                    tv = 4 * hh + kr
                    pts = PT[:, (wi % 2) * KB + 512 * ti:(wi % 2) * KB + 512 * (ti + 1)]
                    first = w["first"] and ti == 0
                    last = w["last"] and ti == 1
                    t.matmul(bank(hb),
                             VSB[:, tv * D + 128 * kj:tv * D + 128 * (kj + 1)],
                             pts, start=first, stop=last)
                    instr = t.matmul(bank(db)[0:1, :], ONES[:, 0:1], pts,
                                     start=first, stop=last)
                p.inc(instr, "aP")

            @block.tensor
            def _(t):
                p.wait(t, "eDVE", mh_eDVE)
                p.wait(t, "sDVE", masks_sDVE)
                for wi, w in enumerate(wins):
                    hh, g, b = w["h"], w["g"], w["b"]
                    if wi >= 2:
                        p.wait(t, "aE", wi - 1)
                    sw = 2 * (wi % 2)
                    for ti, (kj, kr) in enumerate(w["tiles"]):
                        lw = KT[:, kj * KB + 512 * hh + 128 * kr:
                                kj * KB + 512 * hh + 128 * (kr + 1)]
                        for j in range(4):
                            qj = 4 * g + j
                            instr = t.matmul(bank(sw + ti)[:, 128 * j:128 * (j + 1)], lw,
                                             QT[:, qj * KB + 512 * hh + 128 * b:
                                                qj * KB + 512 * hh + 128 * (b + 1)],
                                             start=True, stop=True)
                    p.inc(instr, "aS")
                    if wi >= 1:
                        pv_emit(t, wi - 1)
                    for bi2 in bcast_at.get(wi, []):
                        db2 = 6 + bi2 % 2
                        p.wait(t, "aR", bi2 + 1)
                        p.inc(t.matmul(bank(db2), ONESF[0:1, 0:128], RCB[0:1, 0:512],
                                       start=True, stop=True), "aB")
                pv_emit(t, NWIN - 1)
                for bi2 in bcast_at.get(NWIN, []):
                    db2 = 6 + bi2 % 2
                    p.wait(t, "aR", bi2 + 1)
                    p.inc(t.matmul(bank(db2), ONESF[0:1, 0:128], RCB[0:1, 0:512],
                                   start=True, stop=True), "aB")


            @block.scalar
            def _(sc):
                for wi, w in enumerate(wins):
                    p.wait(sc, "aS", wi + 1)
                    if wi >= 2:
                        p.wait(sc, "aP", wi - 1)
                    sw = 2 * (wi % 2)
                    p.inc(sc.activation(PT[:, (wi % 2) * KB:(wi % 2 + 1) * KB],
                                        psA[:, 512 * sw:512 * sw + 1024], AF.Exp,
                                        bias=MH[:], scale=1.0 / 128.0), "aE")
                    for bi in epi_at.get(wi, []):
                        db = 6 + bi % 2
                        p.wait(sc, "aP", blk_lastwin[bi] + 1)
                        if bi >= 1:
                            p.wait(sc, "aV", bi)
                        sc.activation(LNT[0:1, :], bank(db)[0:1, :], AF.Ln)
                        p.inc(sc.activation(RCB[0:1, :], LNT[0:1, :], AF.Exp,
                                            scale=-1.0), "aR")

            evac_at = {}
            for bi in range(32):
                evac_at.setdefault(min(blk_lastwin[bi] + 3, NWIN - 1), []).append(bi)

            @block.vector
            def _(v):
                for wi, w in enumerate(wins):
                    if win_masked(w):
                        p.wait(v, "aE", wi + 1)
                        for ti, (kj, kr) in enumerate(w["tiles"]):
                            if kr == w["b"]:
                                r = min(max(kj - 4 * w["g"], 0), 4)
                                pts = PT[:, (wi % 2) * KB + 512 * ti:
                                         (wi % 2) * KB + 512 * (ti + 1)]
                                instr = v.tensor_tensor(
                                    pts, pts, MSK[:, r * 512:(r + 1) * 512], ALU.mult)
                        p.inc(instr, "aM")
                    for bi in evac_at.get(wi, []):
                        hb, db2 = 4 + bi % 2, 6 + bi % 2
                        p.wait(v, "aB", bi + 1)
                        if bi >= 2:
                            prev = bi - 2
                            p.wait(v, "dYA%d%d" % (prev // 16, prev % 2),
                                   16 * ((prev % 16) // 2 + 1))
                        p.inc(v.tensor_copy(RCB[:], bank(db2)), "aL")
                        p.inc(v.tensor_tensor(
                            AGS[:, (bi % 2) * 512:(bi % 2 + 1) * 512],
                            bank(hb), RCB[:], ALU.mult), "aV")

            @block.sync
            def _(s):
                for bi, (hh, g, b) in enumerate(att_blocks):
                    p.wait(s, "aV", bi + 1)
                    off = (4 * g) * 512 + 128 * b
                    out_ap = bass.AP(ag_in0 if hh == 0 else ag_in1, off,
                                     [[SEQ, 128], [512, 4], [1, 128]])
                    p.inc(s.dma_start(
                        out=out_ap, in_=AGS[:, (bi % 2) * 512:(bi % 2 + 1) * 512]),
                        "dYA%d%d" % (hh, bi % 2), 16)

        if on("exch"):
            @block.gpsimd
            def _(g):
                p.wait(g, "dYA00", p.cnt["dYA00"])
                p.wait(g, "dYA01", p.cnt["dYA01"])
                p.inc(g.collective_compute(
                    "AllGather", ALU.bypass, replica_groups=[list(range(NC))],
                    ins=[ag_in0.ap().opt()], outs=[ag_out0.ap().opt()]), "cCC")
                p.wait(g, "dYA10", p.cnt["dYA10"])
                p.wait(g, "dYA11", p.cnt["dYA11"])
                p.inc(g.collective_compute(
                    "AllGather", ALU.bypass, replica_groups=[list(range(NC))],
                    ins=[ag_in1.ap().opt()], outs=[ag_out1.ap().opt()]), "cCC")
                p.wait(g, "dG", 16)
                with g.register("r0") as r0:
                    p.wait(g, "cCC", 1)
                    g.reg_load(r0, CIDS[0:1, 0:1])
                    g.reg_mul(r0, r0, 64)
                    for s2 in range(8):
                        in_ap = bass.AP(ag_out0, r0, [[SEQ, 128], [512, 16], [1, 64]])
                        p.inc(g.dma_start(out=GT[:, (2 * s2) * KB:(2 * s2 + 1) * KB],
                                          in_=in_ap), "dG", 16)
                        if s2 < 7:
                            g.reg_add(r0, r0, 128 * SEQ)
                with g.register("r1") as r1:
                    p.wait(g, "cCC", 2)
                    g.reg_load(r1, CIDS[0:1, 0:1])
                    g.reg_mul(r1, r1, 64)
                    for s2 in range(8):
                        in_ap = bass.AP(ag_out1, r1, [[SEQ, 128], [512, 16], [1, 64]])
                        p.inc(g.dma_start(out=GT[:, (2 * s2 + 1) * KB:(2 * s2 + 2) * KB],
                                          in_=in_ap), "dG", 16)
                        if s2 < 7:
                            g.reg_add(r1, r1, 128 * SEQ)

        gt_dG = p.cnt["dG"]

        if on("wo"):
            bW = {k: p.cnt[k] for k in ("sPE", "eDVE", "rB")}
            for j in range(32):
                do, pos = j // 8, j % 8
                if pos < 7:
                    ringB.mark("rB", bW["rB"] + do * 7 + pos + 1)
                else:
                    ringB.mark("sPE", bW["sPE"] + do + 1)

            @block.gpsimd
            def _(g):
                for t in range(32):
                    ringB.load(g, wo_g.ap()[t])

            @block.tensor
            def _(t):
                di_order = [0, 2, 4, 6, 8, 10, 12, 14, 1, 3, 5, 7, 9, 11, 13, 15]
                for do in range(4):
                    if do >= 1:
                        p.wait(t, "eDVE", bW["eDVE"] + 8 * do)
                    for ii, di in enumerate(di_order):
                        p.wait(t, "dG", 16 * (9 if ii < 8 else 17))
                        w = ringB.use(t)
                        for tt in range(8):
                            instr = t.matmul(
                                bank(tt),
                                GT[:, di * KB + 128 * tt:di * KB + 128 * (tt + 1)],
                                w, start=(ii == 0), stop=(ii == 15))
                        if ii % 2 == 1 and ii != 15:
                            p.inc(instr, "rB")
                    p.inc(instr, "sPE")

            @block.vector
            def _(v):
                for do in range(4):
                    p.wait(v, "sPE", bW["sPE"] + do + 1)
                    for tt in range(8):
                        p.inc(v.tensor_copy(
                            X1[:, tt * D + 512 * do:tt * D + 512 * (do + 1)],
                            bank(tt)), "eDVE")

        wo_eDVE = p.cnt["eDVE"]

        if on("ln2"):
            bL2 = {k: p.cnt[k] for k in ("eDVE", "sACT", "sDVE", "sPE")}
            e0, s0 = bL2["eDVE"], bL2["sPE"]

            @block.sync
            def _(s):
                for k in range(8):
                    if k >= 3:
                        p.wait(s, "eDVE", e0 + 6 * (k - 2) - 3)
                    elif k == 2:
                        p.wait(s, "eDVE", e0 + 1)
                    else:
                        p.wait(s, "eDVE", ln1_eDVE)
                    nX[k % 2] += 1
                    p.inc(s.dma_start(out=XST[:, (k % 2) * D:(k % 2 + 1) * D],
                                      in_=xr.ap()[128 * k:128 * (k + 1), :]),
                          "dX%d" % (k % 2), 16)

            for k in range(8):
                x1t = X1[:, k * D:(k + 1) * D]

                @block.vector
                def _(v, k=k, x1t=x1t):
                    p.wait(v, "dX%d" % (k % 2), 16 * (4 + k // 2 + 1))
                    p.wait(v, "eDVE", wo_eDVE - 8 + k + 1)
                    nv = p.inc(v.tensor_tensor(
                        x1t, x1t, XST[:, (k % 2) * D:(k % 2 + 1) * D], ALU.add),
                        "eDVE")
                    assert nv == ((e0 + 1) if k == 0 else (e0 + 6 * k - 3))

                if k >= 1:
                    emit_ln_copies(k - 1, 128 * (k - 1),
                                   dict(sPE0=s0 + 4 * (k - 1),
                                        copies_eDVE=e0 + 6 * (k - 1) + 3))

                @block.sync
                def _(s, k=k, x1t=x1t):
                    p.wait(s, "eDVE", (e0 + 1) if k == 0 else (e0 + 6 * k - 3))
                    p.inc(s.dma_start(out=x1s.ap()[128 * k:128 * (k + 1), :],
                                      in_=x1t), "dSP", 16)

                emit_ln_tile(
                    k, x1t, 128 * k,
                    dict(src_waits=[("eDVE", (e0 + 1) if k == 0 else (e0 + 6 * k - 3))],
                         sq_guard=(e0 + 6 * (k - 2) + 2) if k >= 2 else 0,
                         sACT0=bL2["sACT"] + 2 * k,
                         sDVE_val=bL2["sDVE"] + k + 1,
                         norm_eDVE=e0 + 6 * k + 2,
                         pe_round_waits=[(e0 + 6 * k) if k >= 1 else 0,
                                         (e0 + 6 * k + 1) if k >= 1 else 0,
                                         e0 + 6 * k + 4,
                                         e0 + 6 * k + 5],
                         sPE0=s0 + 4 * k))
            emit_ln_copies(7, 128 * 7, dict(sPE0=s0 + 28, copies_eDVE=e0 + 44))

        ln2_eDVE = p.cnt["eDVE"]
        spill_dSP = p.cnt["dSP"]

        def ffn_12(asrc, dst, bias_sb, nk, wsrc, ngroups):
            b0 = {k: p.cnt[k] for k in ("sPE", "eACT", "rA")}
            gper = nk // 8
            for j in range(ngroups):
                m, pos = j // gper, j % gper
                if pos < gper - 1:
                    ringA.mark("rA", b0["rA"] + m * (gper - 1) + pos + 1)
                else:
                    ringA.mark("sPE", b0["sPE"] + m + 1)

            @block.gpsimd
            def _(g):
                for t in range(ngroups):
                    ringA.load(g, wsrc.ap()[t])

            @block.tensor
            def _(t):
                p.wait(t, "eDVE", ln2_eDVE)
                p.wait(t, "dSP", spill_dSP)
                for m in range(32):
                    pr2 = m % 2
                    p.wait(t, "eACT", b0["eACT"] + max(0, m - 1))
                    for kc in range(nk):
                        w = ringA.use(t)
                        t.matmul(bank(0 + 2 * pr2), w, asrc[:, kc * KB:kc * KB + 512],
                                 start=(kc == 0), stop=(kc == nk - 1))
                        instr = t.matmul(bank(1 + 2 * pr2), w,
                                         asrc[:, kc * KB + 512:(kc + 1) * KB],
                                         start=(kc == 0), stop=(kc == nk - 1))
                        if kc % 8 == 7 and kc != nk - 1:
                            p.inc(instr, "rA")
                    p.inc(instr, "sPE")

            @block.scalar
            def _(sc):
                p.wait(sc, "dSP", spill_dSP)
                p.wait(sc, "dC", 32)
                for m in range(32):
                    pr2 = m % 2
                    p.wait(sc, "sPE", b0["sPE"] + m + 1)
                    sc.activation(dst[:, m * KB:m * KB + 512], bank(0 + 2 * pr2),
                                  AF.Relu, bias=bias_sb[:, m:m + 1])
                    p.inc(sc.activation(dst[:, m * KB + 512:(m + 1) * KB],
                                        bank(1 + 2 * pr2), AF.Relu,
                                        bias=bias_sb[:, m:m + 1]), "eACT")

        if on("ffn1"):
            ffn_12(HT, Z1, B1S, 16, w1_g, 64)
        if on("ffn2"):
            ffn_12(Z1, Z2, B2S, 32, w2_g, 128)

        if on("ffn3"):
            bF3 = {k: p.cnt[k] for k in ("sPE", "eDVE", "dYY", "eACT")}
            nB = [0, 0]
            nXf = [nX[0], nX[1]]
            bF3["rB"] = p.cnt["rB"]
            for j in range(64):
                do, pos = j // 16, j % 16
                if pos < 15:
                    ringB.mark("rB", bF3["rB"] + do * 15 + pos + 1)
                else:
                    ringB.mark("sPE", bF3["sPE"] + do + 1)

            @block.gpsimd
            def _(g):
                for t in range(64):
                    ringB.load(g, w3_g.ap()[t])

            X1R4 = [X1R[:, 0:512], X1R[:, 512:1024], B3S[:, 0:512], B3S[:, 512:1024],
                    XST[:, 1536:2048], X1T0[:], X1T1[:], X1T2[:]]
            B3F = XST[:, 2048:4096]
            FT3 = XST[:, 1024:1536]

            @block.sync
            def _(s):
                p.wait(s, "dSP", spill_dSP)
                p.inc(s.dma_start(out=B3F, in_=b3d.ap()), "dB0", 16)
                q = 0
                for do in range(4):
                    for tt in range(8):
                        if q >= 8:
                            p.wait(s, "eDVE", bF3["eDVE"] + q - 7)
                        p.inc(s.dma_start(
                            out=X1R4[q % 8],
                            in_=x1s.ap()[128 * tt:128 * (tt + 1),
                                         512 * do:512 * (do + 1)]),
                            "dR%d" % (q % 8), 16)
                        if q >= 1:
                            qs = q - 1
                            p.wait(s, "eDVE", bF3["eDVE"] + qs + 1)
                            p.inc(s.dma_start(
                                out=y.ap()[128 * (qs % 8):128 * (qs % 8 + 1),
                                           512 * (qs // 8):512 * (qs // 8 + 1)],
                                in_=YST[:, (qs % 2) * 512:(qs % 2 + 1) * 512]),
                                "dY%d" % (qs % 2), 16)
                        q += 1
                p.wait(s, "eDVE", bF3["eDVE"] + 32)
                p.inc(s.dma_start(
                    out=y.ap()[128 * 7:128 * 8, 512 * 3:512 * 4],
                    in_=YST[:, 512:1024]), "dY1", 16)

            @block.tensor
            def _(t):
                p.wait(t, "eACT", p.cnt["eACT"])
                for do in range(4):
                    if do >= 1:
                        p.wait(t, "eDVE", bF3["eDVE"] + 8 * do)
                    for kc in range(32):
                        w = ringB.use(t)
                        for tt in range(8):
                            instr = t.matmul(
                                bank(tt),
                                Z2[:, kc * KB + 128 * tt:kc * KB + 128 * (tt + 1)],
                                w, start=(kc == 0), stop=(kc == 31))
                        if kc % 2 == 1 and kc != 31:
                            p.inc(instr, "rB")
                    p.inc(instr, "sPE")

            @block.vector
            def _(v):
                q = 0
                p.wait(v, "dB0", 16)
                for do in range(4):
                    p.wait(v, "sPE", bF3["sPE"] + do + 1)
                    for tt in range(8):
                        p.wait(v, "dR%d" % (q % 8), 16 * (q // 8 + 1))
                        if q >= 2:
                            p.wait(v, "dY%d" % (q % 2), 16 * ((q - 2) // 2 + 1))
                        v.tensor_tensor(FT3, bank(tt), X1R4[q % 8], ALU.add)
                        p.inc(v.tensor_tensor(
                            YST[:, (q % 2) * 512:(q % 2 + 1) * 512], FT3,
                            B3F[:, 512 * do:512 * (do + 1)], ALU.add),
                            "eDVE")
                        q += 1

        sb_map = {"hT": HT, "QT": QT, "KT": KT, "V": VSB, "GT": GT, "h2T": HT,
                  "x1": X1, "z1T": Z1, "z2T": Z2, "mh": MH}

        @block.gpsimd
        def _(g):
            for nm in ["eDVE", "eACT", "sACT", "sPE", "aV", "aP", "aL", "dG",
                       "dSP", "dYY", "dY0", "dY1",
                       "dYA00", "dYA01", "dYA10", "dYA11"]:
                p.wait(g, nm, p.cnt[nm])
            for k in debug:
                if k == "agin":
                    p.inc(g.dma_start(out=GT[:, 0:SEQ], in_=ag_in0.ap()), "dYY", 16)
                    p.inc(g.dma_start(out=GT[:, SEQ:2 * SEQ],
                                      in_=ag_in1.ap()), "dYY", 16)
                    p.wait(g, "dYY", p.cnt["dYY"])
                    instr = g.dma_start(out=dbg[k].ap(), in_=GT[:, 0:2 * SEQ])
                else:
                    instr = g.dma_start(out=dbg[k].ap(), in_=sb_map[k][:])
                p.inc(instr, "dYY", 16)
            p.wait(g, "dYY", p.cnt["dYY"])

        @block.sync
        def _(s):
            p.wait(s, "dYY", p.cnt["dYY"])
            p.wait(s, "dY0", p.cnt["dY0"])
            p.wait(s, "dY1", p.cnt["dY1"])

    return nc


_CACHE = {}


def _perm(c):
    t = np.arange(1024)
    return 16 * (64 * c + (t % 64)) + (t // 64)


WO_PERM = [0, 2, 4, 6, 8, 10, 12, 14, 1, 3, 5, 7, 9, 11, 13, 15]


def _pretile(w, kchunks, mchunks, mwidth, group, kperm=None):
    wt = np.ascontiguousarray(np.asarray(w, np.float32).T)
    a = wt.reshape(kchunks, 128, mchunks, mwidth).transpose(2, 0, 1, 3)
    tiles = a.reshape(mchunks * kchunks, 128, mwidth)
    if kperm is not None:
        tiles = tiles.reshape(mchunks, kchunks, 128, mwidth)[:, kperm]
        tiles = tiles.reshape(mchunks * kchunks, 128, mwidth)
    gshape = (tiles.shape[0] // group, group, 128, mwidth)
    gg = tiles.reshape(gshape).transpose(0, 2, 1, 3)
    return np.ascontiguousarray(gg.reshape(gshape[0], 128, group * mwidth))


def kernel(L0_attn_x, L0_attn_Wq, L0_attn_Wk, L0_attn_Wv, L0_attn_Wo,
           L0_ffn_W1, L0_ffn_b1, L0_ffn_W2, L0_ffn_b2, L0_ffn_W3, L0_ffn_b3):
    x = np.asarray(L0_attn_x, dtype=np.float32)[0]
    if "nc" not in _CACHE:
        _CACHE["nc"] = build_program()
    nc = _CACHE["nc"]
    wq = _pretile(L0_attn_Wq, 16, 16, 128, 8)
    wk = _pretile(L0_attn_Wk, 16, 16, 128, 8)
    wv = _pretile(L0_attn_Wv, 16, 4, 512, 2)
    wo = _pretile(L0_attn_Wo, 16, 4, 512, 2, WO_PERM)
    w1 = _pretile(L0_ffn_W1, 16, 32, 128, 8)
    w2 = _pretile(L0_ffn_W2, 32, 32, 128, 8)
    w3 = _pretile(L0_ffn_W3, 32, 4, 512, 2)
    b1 = np.ascontiguousarray(np.asarray(L0_ffn_b1, np.float32).reshape(32, 128).T)
    b2 = np.ascontiguousarray(np.asarray(L0_ffn_b2, np.float32).reshape(32, 128).T)
    b3 = np.ascontiguousarray(
        np.broadcast_to(np.asarray(L0_ffn_b3, np.float32), (128, D)))
    in_maps, perms = [], []
    for c in range(NC):
        pc = _perm(c)
        perms.append(pc)
        in_maps.append(dict(
            xq=np.ascontiguousarray(x[1024 * c:1024 * (c + 1)]),
            xr=np.ascontiguousarray(x[pc]),
            cid=np.array([[c]], dtype=np.int32),
            wq_g=wq, wk_g=wk, wv_g=wv, wo_g=wo, w1_g=w1, w2_g=w2, w3_g=w3,
            b1d=b1, b2d=b2, b3d=b3))
    res = run_bass_kernel_spmd(nc, in_maps, core_ids=list(range(NC)))
    out = np.empty((SEQ, D), dtype=np.float32)
    for c in range(NC):
        out[perms[c]] = res.results[c]["y"]
    return out.reshape(1, SEQ, D)


# version 10
# speedup vs baseline: 1.0388x; 1.0145x over previous
import contextlib
import numpy as np
import ml_dtypes
import concourse.bass as bass
import concourse.mybir as mybir
from concourse.bass_utils import run_bass_kernel_spmd

F32 = mybir.dt.float32
BF16 = mybir.dt.bfloat16
I32 = mybir.dt.int32
AF = mybir.ActivationFunctionType
ALU = mybir.AluOpType
AX = mybir.AxisListType

D = 2048
SEQ = 8192
NC = 8
KB = 1024
PHASES = ("ln1", "qkv", "mh", "attn", "exch", "wo", "ln2", "ffn1", "ffn2", "ffn3")


class P:
    def __init__(self, nc, st):
        self.nc, self.st = nc, st
        self.sem, self.cnt = {}, {}

    def mksem(self, name):
        self.sem[name] = self.st.enter_context(self.nc.semaphore(name))
        self.cnt[name] = 0

    def inc(self, instr, name, n=1):
        instr.then_inc(self.sem[name], n)
        self.cnt[name] += n
        return self.cnt[name]

    def wait(self, eng, name, val):
        if val > 0:
            eng.wait_ge(self.sem[name], val)


def _nops(eng, n=6):
    for _ in range(n):
        eng.engine_nop()


class Ring:

    def __init__(self, p, name, w, group, sb):
        self.p, self.name, self.w, self.group, self.sb = p, name, w, group, sb
        self.gw = w * group
        self.nload = 0
        self.ncons = 0
        self.marks = []
        for s in range(3):
            p.mksem(f"l{name}{s}")

    def mark(self, sem, val):
        self.marks.append((sem, val))

    def load(self, g, dram_ap):
        m = self.nload
        if m >= 3:
            sem, val = self.marks[m - 3]
            self.p.wait(g, sem, val)
        s = m % 3
        slot = self.sb[:, s * self.gw:(s + 1) * self.gw]
        self.p.inc(g.dma_start(out=slot, in_=dram_ap), f"l{self.name}{s}", 16)
        self.nload += 1

    def use(self, eng):
        k = self.ncons
        m, i = k // self.group, k % self.group
        if i == 0:
            self.p.wait(eng, f"l{self.name}{m % 3}", 16 * (m // 3 + 1))
        base = (m % 3) * self.gw + i * self.w
        self.ncons += 1
        return self.sb[:, base:base + self.w]


def build_program(upto="ffn3", debug=()):
    sub = None
    if upto in ("qkva", "qkvb"):
        sub, upto = upto, "qkv"
    nc = bass.Bass("TRN2", target_bir_lowering=False, debug=False,
                   detect_race_conditions=False)
    nup = PHASES.index(upto)

    def on(ph):
        return PHASES.index(ph) <= nup

    xq = nc.dram_tensor("xq", [1024, D], F32, kind="ExternalInput")
    xr = nc.dram_tensor("xr", [1024, D], F32, kind="ExternalInput")
    cid = nc.dram_tensor("cid", [1, 1], I32, kind="ExternalInput")
    wq_g = nc.dram_tensor("wq_g", [32, 128, 1024], F32, kind="ExternalInput")
    wk_g = nc.dram_tensor("wk_g", [32, 128, 1024], F32, kind="ExternalInput")
    wv_g = nc.dram_tensor("wv_g", [32, 128, 1024], F32, kind="ExternalInput")
    wo_g = nc.dram_tensor("wo_g", [32, 128, 1024], F32, kind="ExternalInput")
    w1_g = nc.dram_tensor("w1_g", [64, 128, 1024], F32, kind="ExternalInput")
    w2_g = nc.dram_tensor("w2_g", [128, 128, 1024], F32, kind="ExternalInput")
    w3_g = nc.dram_tensor("w3_g", [64, 128, 1024], F32, kind="ExternalInput")
    b1d = nc.dram_tensor("b1d", [128, 32], F32, kind="ExternalInput")
    b2d = nc.dram_tensor("b2d", [128, 32], F32, kind="ExternalInput")
    b3d = nc.dram_tensor("b3d", [128, D], F32, kind="ExternalInput")
    y = nc.dram_tensor("y", [1024, D], F32, kind="ExternalOutput")
    ag_in0 = nc.dram_tensor("ag_in0", [128, SEQ], BF16)
    ag_in1 = nc.dram_tensor("ag_in1", [128, SEQ], BF16)
    ag_out0 = nc.dram_tensor("ag_out0", [NC * 128, SEQ], BF16, addr_space="Shared")
    ag_out1 = nc.dram_tensor("ag_out1", [NC * 128, SEQ], BF16, addr_space="Shared")
    x1s = nc.dram_tensor("x1s", [1024, D], F32)
    dbg_shapes = {"hT": [128, 16 * KB], "QT": [128, 16 * KB],
                  "KT": [128, 16 * KB], "V": [128, 8 * D],
                  "GT": [128, 16 * KB], "h2T": [128, 16 * KB],
                  "x1": [128, 16 * KB], "z1T": [128, 32 * KB],
                  "z2T": [128, 32 * KB], "agin": [128, 2 * SEQ],
                  "mh": [128, 1]}
    dbg = {k: nc.dram_tensor("dbg_" + k, dbg_shapes[k], F32, kind="ExternalOutput")
           for k in debug}

    st = contextlib.ExitStack()
    with st:
        SB_BASE = 20480

        def place(name, shape, dtype, off):
            return nc.alloc_sbuf_tensor_at(name, shape, dtype, offset=SB_BASE + off)

        HT = place("HT", [128, 16 * KB], BF16, 0)
        QT = place("QT", [128, 16 * KB], BF16, 32768)
        KT = place("KT", [128, 16 * KB], BF16, 65536)
        X1 = place("X1", [128, 16 * KB], F32, 32768)
        Z1 = place("Z1", [128, 32 * KB], BF16, 32768)
        VSB = place("VSB", [128, 8 * D], BF16, 98304)
        Z2 = place("Z2", [128, 32 * KB], BF16, 98304)
        GT = place("GT", [128, 16 * KB], BF16, 131072)
        MSK = place("MSK", [128, 5 * 512], BF16, 131072)
        HSB = place("HSB", [128, D], BF16, 136192)
        HSB2 = place("HSB2", [128, D], BF16, 140288)
        PT = place("PT", [128, 2 * KB], BF16, 144384)
        AGS = place("AGS", [128, 2 * 512], BF16, 148480)
        RCB = place("RCB", [128, 512], F32, 150528)
        LNT = place("LNT", [128, 512], F32, 152576)
        XST = place("XST", [128, 2 * D], F32, 163840)
        X1R = place("X1R", [128, 2 * 512], F32, 180224)
        B3S = place("B3S", [128, 2 * 512], F32, 184320)
        RA = place("RA", [128, 3 * 8 * 128], BF16, 188416)
        RB = place("RB", [128, 3 * 2 * 512], BF16, 194560)
        SM = place("SM", [128, 64], F32, 200704)
        SMB = place("SMB", [128, 16], BF16, 200960)
        IDNB = place("IDNB", [128, 128], BF16, 201024)
        ONES = place("ONES", [128, 128], BF16, 201536)
        B1S = place("B1S", [128, 32], F32, 201792)
        B2S = place("B2S", [128, 32], F32, 201920)
        MH = place("MH", [128, 1], F32, 202048)
        CIDS = place("CIDS", [1, 1], I32, 202080)
        ONESF = place("ONESF", [128, 128], F32, 202112)
        X1T0 = place("X1T0", [128, 512], F32, 202624)
        X1T1 = place("X1T1", [128, 512], F32, 204672)
        X1T2 = place("X1T2", [128, 512], F32, 206720)
        YST = XST

        psA = st.enter_context(nc.psum_tensor("psA", [128, 4096], F32))

        def bank(b):
            return psA[:, 512 * b:512 * (b + 1)]

        def bankb(b):
            return psA[:, 512 * b:512 * (b + 1)].bitcast(BF16)

        p = P(nc, st)
        for s in ["dX0", "dX1", "dB0", "dB1", "dC", "dG", "dSP", "dYY",
                  "dYA00", "dYA01", "dYA10", "dYA11",
                  "sPE", "sACT", "sDVE", "eDVE", "eACT",
                  "aS", "aE", "aM", "aP", "aL", "aR", "aV", "aB", "cCC", "rA", "rB",
                  "dY0", "dY1", "dR0", "dR1", "dR2", "dR3",
                  "dR4", "dR5", "dR6", "dR7", "dW0", "dW1"]:
            p.mksem(s)
        ringA = Ring(p, "A", 128, 8, RA)
        ringB = Ring(p, "B", 512, 2, RB)

        block = st.enter_context(nc.Block())

        att_blocks = [(h, g, b) for h in range(2) for g in range(4) for b in range(4)]
        wins = []
        for bi, (h, g, b) in enumerate(att_blocks):
            tl = [(kj, kr) for kj in range(16) for kr in range(b + 1)]
            for w in range(len(tl) // 2):
                wins.append(dict(bi=bi, h=h, g=g, b=b,
                                 tiles=(tl[2 * w], tl[2 * w + 1]),
                                 first=(w == 0), last=(w == len(tl) // 2 - 1)))
        NWIN = len(wins)
        blk_lastwin = {}
        for wi, w in enumerate(wins):
            blk_lastwin[w["bi"]] = wi

        def win_masked(w):
            return any(kr == w["b"] for kj, kr in w["tiles"])

        nmask_upto = []
        c_ = 0
        for w in wins:
            c_ += 1 if win_masked(w) else 0
            nmask_upto.append(c_)

        @block.gpsimd
        def _(g):
            g.memset(ONES[:], 1.0)
            g.memset(ONESF[:], 1.0)
            g.memset(SMB[:], 1.0)
            g.memset(SM[:], 0.0)
            g.iota(IDNB[:], [[1, 128]], base=0, channel_multiplier=-1,
                   allow_small_or_imprecise_dtypes=True)
            instr = None
            for r in range(5):
                for j in range(4):
                    instr = g.iota(MSK[:, r * 512 + j * 128:r * 512 + (j + 1) * 128],
                                   [[1, 128]], base=(0 if j >= r else -1),
                                   channel_multiplier=-1,
                                   allow_small_or_imprecise_dtypes=True)
            p.inc(instr, "sDVE")
            p.inc(g.dma_start(out=CIDS[:], in_=cid.ap()), "dG", 16)

        setup_sDVE = p.cnt["sDVE"]

        @block.vector
        def _(v):
            p.wait(v, "sDVE", setup_sDVE)
            for r in range(5):
                v.tensor_scalar(MSK[:, r * 512:(r + 1) * 512],
                                MSK[:, r * 512:(r + 1) * 512], 0.0, None, ALU.is_ge)
            p.inc(v.tensor_scalar(IDNB[:], IDNB[:], 0.0, None, ALU.is_equal), "sDVE")

        masks_sDVE = p.cnt["sDVE"]

        def emit_ln_tile(k, src_ap, ht_tokbase, W):
            HSBk = HSB if k % 2 == 0 else HSB2
            """Emit LN + transpose of one 128-token tile into HT (bf16).

            W (absolute semaphore values):
              src_waits: [(sem, val)] before touching src (scalar AND vector)
              sq_guard:  eDVE value guarding SM parity slots (0 = skip)
              sACT0:     sACT count before this tile's two incs
              sDVE_val:  sDVE count after this tile's var inc
              norm_eDVE: eDVE count after this tile's normalize inc
              pe_round_waits: 4 absolute eDVE values (0 = skip) per PE round
              sPE0:      sPE count before this tile's 4 round incs
            Emits: sACT +2, sDVE +1, eDVE +5 (normalize + 4 copy rounds),
            sPE +4.
            """
            pr = (k % 2) * 16

            @block.scalar
            def _(sc):
                for sem, val in W["src_waits"]:
                    p.wait(sc, sem, val)
                if W["sq_guard"]:
                    p.wait(sc, "eDVE", W["sq_guard"])
                p.wait(sc, "sPE", max(W["sPE0"] - 4, 0))
                nv = p.inc(sc.activation(HSBk[:], src_ap, AF.Square,
                                         accum_out=SM[:, pr + 1:pr + 2]), "sACT")
                assert nv == W["sACT0"] + 1

            @block.vector
            def _(v):
                for sem, val in W["src_waits"]:
                    p.wait(v, sem, val)
                if W["sq_guard"]:
                    p.wait(v, "eDVE", W["sq_guard"])
                v.tensor_reduce(SM[:, pr + 0:pr + 1], src_ap, AX.X, ALU.add)
                p.wait(v, "sACT", W["sACT0"] + 1)
                _nops(v)
                v.tensor_scalar(SM[:, pr + 3:pr + 4], SM[:, pr + 0:pr + 1],
                                -1.0 / D, None, ALU.mult)
                _nops(v)
                v.tensor_tensor(SM[:, pr + 4:pr + 5], SM[:, pr + 3:pr + 4],
                                SM[:, pr + 3:pr + 4], ALU.mult)
                v.tensor_scalar(SM[:, pr + 5:pr + 6], SM[:, pr + 1:pr + 2],
                                1.0 / D, None, ALU.mult)
                _nops(v)
                v.tensor_tensor(SM[:, pr + 6:pr + 7], SM[:, pr + 5:pr + 6],
                                SM[:, pr + 4:pr + 5], ALU.subtract)
                _nops(v)
                nv = p.inc(v.tensor_scalar(SM[:, pr + 9:pr + 10],
                                           SM[:, pr + 6:pr + 7], 1e-5, None,
                                           ALU.add), "sDVE")
                assert nv == W["sDVE_val"]

            @block.scalar
            def _(sc):
                p.wait(sc, "sDVE", W["sDVE_val"])
                sc.activation(SM[:, pr + 7:pr + 8], SM[:, pr + 9:pr + 10], AF.Ln)
                nv = p.inc(sc.activation(SM[:, pr + 8:pr + 9], SM[:, pr + 7:pr + 8],
                                         AF.Exp, scale=-0.5), "sACT")
                assert nv == W["sACT0"] + 2

            @block.vector
            def _(v):
                p.wait(v, "sACT", W["sACT0"] + 2)
                _nops(v)
                nv = p.inc(v.tensor_scalar(HSBk[:], src_ap, SM[:, pr + 3:pr + 4],
                                           SM[:, pr + 8:pr + 9], ALU.add,
                                           ALU.mult), "eDVE")
                assert nv == W["norm_eDVE"]

            @block.tensor
            def _(t):
                p.wait(t, "sDVE", masks_sDVE)
                p.wait(t, "eDVE", W["norm_eDVE"])
                for r in range(4):
                    if W["pe_round_waits"][r]:
                        p.wait(t, "eDVE", W["pe_round_waits"][r])
                    for s4 in range(4):
                        jj = 4 * r + s4
                        b = 4 * (r % 2) + s4
                        instr = t.transpose(bankb(b)[:, 0:128],
                                            HSBk[:, 128 * jj:128 * (jj + 1)], IDNB[:])
                    nv = p.inc(instr, "sPE")
                    assert nv == W["sPE0"] + r + 1


        def emit_ln_copies(k, ht_tokbase, W):
            @block.vector
            def _(v):
                for r in range(4):
                    p.wait(v, "sPE", W["sPE0"] + r + 1)
                    for s4 in range(4):
                        jj = 4 * r + s4
                        b = 4 * (r % 2) + s4
                        instr = v.tensor_copy(
                            HT[:, jj * KB + ht_tokbase:jj * KB + ht_tokbase + 128],
                            bankb(b)[:, 0:128])
                    nv = p.inc(instr, "eDVE")
                    assert nv == W["copies_eDVE"] + r + 1

        b_ln1 = {k: p.cnt[k] for k in ("eDVE", "sACT", "sDVE", "sPE")}
        nX = [0, 0]

        @block.sync
        def _(s):
            p.inc(s.dma_start(out=B1S[:], in_=b1d.ap()), "dC", 16)
            p.inc(s.dma_start(out=B2S[:], in_=b2d.ap()), "dC", 16)
            for k in range(8):
                if k >= 2:
                    p.wait(s, "eDVE", b_ln1["eDVE"] + 5 * (k - 2) + 1)
                nX[k % 2] += 1
                p.inc(s.dma_start(out=XST[:, (k % 2) * D:(k % 2 + 1) * D],
                                  in_=xq.ap()[128 * k:128 * (k + 1), :]),
                      "dX%d" % (k % 2), 16)

        if on("ln1"):
            e0, s0 = b_ln1["eDVE"], b_ln1["sPE"]
            for k in range(8):
                if k >= 1:
                    emit_ln_copies(k - 1, 128 * (k - 1),
                                   dict(sPE0=s0 + 4 * (k - 1),
                                        copies_eDVE=e0 + 5 * (k - 1) + 1))
                emit_ln_tile(
                    k, XST[:, (k % 2) * D:(k % 2 + 1) * D], 128 * k,
                    dict(src_waits=[("dX%d" % (k % 2), 16 * (k // 2 + 1))],
                         sq_guard=(e0 + 5 * (k - 2) + 1) if k >= 2 else 0,
                         sACT0=b_ln1["sACT"] + 2 * k,
                         sDVE_val=b_ln1["sDVE"] + k + 1,
                         norm_eDVE=e0 + 5 * k + 1,
                         pe_round_waits=[(e0 + 5 * k - 1) if k >= 1 else 0,
                                         (e0 + 5 * k) if k >= 1 else 0,
                                         e0 + 5 * k + 2,
                                         e0 + 5 * k + 3],
                         sPE0=s0 + 4 * k))
            emit_ln_copies(7, 128 * 7, dict(sPE0=s0 + 28, copies_eDVE=e0 + 36))
        ln1_eDVE = p.cnt["eDVE"]

        if on("qkv"):
            def proj_qk(dst, wsrc):
                b0 = {k: p.cnt[k] for k in ("sPE", "eDVE", "rA")}
                for j in range(32):
                    if j % 2 == 0:
                        ringA.mark("rA", b0["rA"] + j // 2 + 1)
                    else:
                        ringA.mark("sPE", b0["sPE"] + j // 2 + 1)

                @block.gpsimd
                def _(g):
                    for t in range(32):
                        ringA.load(g, wsrc.ap()[t])

                @block.tensor
                def _(t):
                    p.wait(t, "eDVE", ln1_eDVE)
                    for dq in range(16):
                        pr2 = dq % 2
                        if dq >= 2:
                            p.wait(t, "eDVE", b0["eDVE"] + dq - 1)
                        for kc in range(16):
                            w = ringA.use(t)
                            t.matmul(bank(4 + 2 * pr2), w, HT[:, kc * KB:kc * KB + 512],
                                     start=(kc == 0), stop=(kc == 15))
                            instr = t.matmul(bank(5 + 2 * pr2), w,
                                             HT[:, kc * KB + 512:(kc + 1) * KB],
                                             start=(kc == 0), stop=(kc == 15))
                            if kc == 7:
                                p.inc(instr, "rA")
                        p.inc(instr, "sPE")

                @block.vector
                def _(v):
                    for dq in range(16):
                        pr2 = dq % 2
                        p.wait(v, "sPE", b0["sPE"] + dq + 1)
                        v.tensor_copy(dst[:, dq * KB:dq * KB + 512], bank(4 + 2 * pr2))
                        p.inc(v.tensor_copy(dst[:, dq * KB + 512:(dq + 1) * KB],
                                            bank(5 + 2 * pr2)), "eDVE")

            proj_qk(QT, wq_g)
            if sub != "qkva":
                proj_qk(KT, wk_g)
            qk_eDVE = p.cnt["eDVE"]
            bV = {k: p.cnt[k] for k in ("sPE", "eDVE", "rB")}
            if sub not in ("qkva", "qkvb"):
                for j in range(32):
                    dv, pos = j // 8, j % 8
                    if pos < 7:
                        ringB.mark("rB", bV["rB"] + dv * 7 + pos + 1)
                    else:
                        ringB.mark("sPE", bV["sPE"] + dv + 1)

                @block.gpsimd
                def _(g):
                    for t in range(32):
                        ringB.load(g, wv_g.ap()[t])

                @block.tensor
                def _(t):
                    p.wait(t, "eDVE", qk_eDVE)
                    for dv in range(4):
                        if dv >= 1:
                            p.wait(t, "eDVE", bV["eDVE"] + dv)
                        for kc in range(16):
                            w = ringB.use(t)
                            for tv in range(8):
                                instr = t.matmul(
                                    bank(tv),
                                    HT[:, kc * KB + 128 * tv:kc * KB + 128 * (tv + 1)],
                                    w, start=(kc == 0), stop=(kc == 15))
                            if kc % 2 == 1 and kc != 15:
                                p.inc(instr, "rB")
                        p.inc(instr, "sPE")

                @block.vector
                def _(v):
                    for dv in range(4):
                        p.wait(v, "sPE", bV["sPE"] + dv + 1)
                        for tv in range(8):
                            instr = v.tensor_copy(
                                VSB[:, tv * D + dv * 512:tv * D + (dv + 1) * 512],
                                bank(tv))
                        p.inc(instr, "eDVE")

        v_eDVE = p.cnt["eDVE"]

        if on("mh"):
            bM = {k: p.cnt[k] for k in ("sDVE", "sPE", "eDVE", "sACT")}

            for m in range(32):
                mat = QT if m < 16 else KT
                ch = m % 16
                run = 40 if m < 16 else 41
                sl = (m % 2) * KB

                @block.vector
                def _(v, m=m, mat=mat, ch=ch, sl=sl):
                    if m == 0:
                        p.wait(v, "eDVE", v_eDVE)
                    if m >= 2:
                        p.wait(v, "sPE", bM["sPE"] + m - 1)
                    p.inc(v.tensor_tensor(PT[:, sl:sl + KB], mat[:, ch * KB:(ch + 1) * KB],
                                          mat[:, ch * KB:(ch + 1) * KB], ALU.mult),
                          "sDVE")

                @block.tensor
                def _(t, m=m, sl=sl):
                    p.wait(t, "sDVE", bM["sDVE"] + m + 1)
                    if m >= 2:
                        p.wait(t, "eDVE", bM["eDVE"] + m - 1)
                    if m == 0:
                        p.wait(t, "sDVE", masks_sDVE)
                    b2a = 2 * (m % 2)
                    t.matmul(bank(b2a), ONES[:], PT[:, sl:sl + 512],
                             start=True, stop=True)
                    p.inc(t.matmul(bank(b2a + 1), ONES[:], PT[:, sl + 512:sl + KB],
                                   start=True, stop=True), "sPE")

                @block.vector
                def _(v, m=m, run=run):
                    p.wait(v, "sPE", bM["sPE"] + m + 1)
                    b2a = 2 * (m % 2)
                    v.tensor_reduce(SM[:, 44:45], bank(b2a), AX.X, ALU.max)
                    v.tensor_reduce(SM[:, 45:46], bank(b2a + 1), AX.X, ALU.max)
                    _nops(v)
                    if m % 16 == 0:
                        v.tensor_tensor(SM[:, run:run + 1], SM[:, 44:45],
                                        SM[:, 45:46], ALU.max)
                    else:
                        v.tensor_tensor(SM[:, 44:45], SM[:, 44:45],
                                        SM[:, 45:46], ALU.max)
                        _nops(v)
                        v.tensor_tensor(SM[:, run:run + 1], SM[:, run:run + 1],
                                        SM[:, 44:45], ALU.max)
                    nv = p.inc(v.engine_nop(), "eDVE")
                    assert nv == bM["eDVE"] + m + 1

            @block.scalar
            def _(sc):
                p.wait(sc, "eDVE", bM["eDVE"] + 32)
                sc.activation(SM[:, 46:47], SM[:, 40:41], AF.Ln)
                sc.activation(SM[:, 46:47], SM[:, 46:47], AF.Exp, scale=0.5)
                sc.activation(SM[:, 47:48], SM[:, 41:42], AF.Ln)
                p.inc(sc.activation(SM[:, 47:48], SM[:, 47:48], AF.Exp, scale=0.5),
                      "sACT")

            @block.vector
            def _(v):
                p.wait(v, "sACT", bM["sACT"] + 1)
                _nops(v)
                v.tensor_tensor(SM[:, 48:49], SM[:, 46:47], SM[:, 47:48], ALU.mult)
                _nops(v)
                p.inc(v.tensor_scalar(MH[:], SM[:, 48:49], -1.0 / 128.0, None,
                                      ALU.mult), "eDVE")

        mh_eDVE = p.cnt["eDVE"]

        if on("attn"):

            epi_at = {}
            for bi in range(32):
                epi_at.setdefault(min(blk_lastwin[bi] + 2, NWIN - 1), []).append(bi)
            bcast_at = {}
            for bi in range(32):
                bcast_at.setdefault(min(blk_lastwin[bi] + 3, NWIN), []).append(bi)

            def pv_emit(t, wi):
                w = wins[wi]
                hh, bi = w["h"], w["bi"]
                if w["first"] and bi >= 2:
                    p.wait(t, "aV", bi - 1)
                    p.wait(t, "aL", bi - 1)
                p.wait(t, "aE", wi + 1)
                if win_masked(w):
                    p.wait(t, "aM", nmask_upto[wi])
                hb, db = 4 + bi % 2, 6 + bi % 2
                for ti, (kj, kr) in enumerate(w["tiles"]):
                    tv = 4 * hh + kr
                    pts = PT[:, (wi % 2) * KB + 512 * ti:(wi % 2) * KB + 512 * (ti + 1)]
                    first = w["first"] and ti == 0
                    last = w["last"] and ti == 1
                    t.matmul(bank(hb),
                             VSB[:, tv * D + 128 * kj:tv * D + 128 * (kj + 1)],
                             pts, start=first, stop=last)
                    instr = t.matmul(bank(db)[0:1, :], ONES[:, 0:1], pts,
                                     start=first, stop=last)
                p.inc(instr, "aP")

            @block.tensor
            def _(t):
                p.wait(t, "eDVE", mh_eDVE)
                p.wait(t, "sDVE", masks_sDVE)
                for wi, w in enumerate(wins):
                    hh, g, b = w["h"], w["g"], w["b"]
                    if wi >= 2:
                        p.wait(t, "aE", wi - 1)
                    sw = 2 * (wi % 2)
                    for ti, (kj, kr) in enumerate(w["tiles"]):
                        lw = KT[:, kj * KB + 512 * hh + 128 * kr:
                                kj * KB + 512 * hh + 128 * (kr + 1)]
                        for j in range(4):
                            qj = 4 * g + j
                            instr = t.matmul(bank(sw + ti)[:, 128 * j:128 * (j + 1)], lw,
                                             QT[:, qj * KB + 512 * hh + 128 * b:
                                                qj * KB + 512 * hh + 128 * (b + 1)],
                                             start=True, stop=True)
                    p.inc(instr, "aS")
                    if wi >= 1:
                        pv_emit(t, wi - 1)
                    for bi2 in bcast_at.get(wi, []):
                        db2 = 6 + bi2 % 2
                        p.wait(t, "aR", bi2 + 1)
                        p.inc(t.matmul(bank(db2), ONESF[0:1, 0:128], RCB[0:1, 0:512],
                                       start=True, stop=True), "aB")
                pv_emit(t, NWIN - 1)
                for bi2 in bcast_at.get(NWIN, []):
                    db2 = 6 + bi2 % 2
                    p.wait(t, "aR", bi2 + 1)
                    p.inc(t.matmul(bank(db2), ONESF[0:1, 0:128], RCB[0:1, 0:512],
                                   start=True, stop=True), "aB")


            @block.scalar
            def _(sc):
                for wi, w in enumerate(wins):
                    p.wait(sc, "aS", wi + 1)
                    if wi >= 2:
                        p.wait(sc, "aP", wi - 1)
                    sw = 2 * (wi % 2)
                    p.inc(sc.activation(PT[:, (wi % 2) * KB:(wi % 2 + 1) * KB],
                                        psA[:, 512 * sw:512 * sw + 1024], AF.Exp,
                                        bias=MH[:], scale=1.0 / 128.0), "aE")
                    for bi in epi_at.get(wi, []):
                        db = 6 + bi % 2
                        p.wait(sc, "aP", blk_lastwin[bi] + 1)
                        if bi >= 1:
                            p.wait(sc, "aV", bi)
                        sc.activation(LNT[0:1, :], bank(db)[0:1, :], AF.Ln)
                        p.inc(sc.activation(RCB[0:1, :], LNT[0:1, :], AF.Exp,
                                            scale=-1.0), "aR")

            evac_at = {}
            for bi in range(32):
                evac_at.setdefault(min(blk_lastwin[bi] + 3, NWIN - 1), []).append(bi)

            @block.vector
            def _(v):
                for wi, w in enumerate(wins):
                    if win_masked(w):
                        p.wait(v, "aE", wi + 1)
                        for ti, (kj, kr) in enumerate(w["tiles"]):
                            if kr == w["b"]:
                                r = min(max(kj - 4 * w["g"], 0), 4)
                                pts = PT[:, (wi % 2) * KB + 512 * ti:
                                         (wi % 2) * KB + 512 * (ti + 1)]
                                instr = v.tensor_tensor(
                                    pts, pts, MSK[:, r * 512:(r + 1) * 512], ALU.mult)
                        p.inc(instr, "aM")
                    for bi in evac_at.get(wi, []):
                        hb, db2 = 4 + bi % 2, 6 + bi % 2
                        p.wait(v, "aB", bi + 1)
                        if bi >= 2:
                            prev = bi - 2
                            p.wait(v, "dYA%d%d" % (prev // 16, prev % 2),
                                   16 * ((prev % 16) // 2 + 1))
                        p.inc(v.tensor_copy(RCB[:], bank(db2)), "aL")
                        p.inc(v.tensor_tensor(
                            AGS[:, (bi % 2) * 512:(bi % 2 + 1) * 512],
                            bank(hb), RCB[:], ALU.mult), "aV")

            @block.sync
            def _(s):
                for bi, (hh, g, b) in enumerate(att_blocks):
                    p.wait(s, "aV", bi + 1)
                    off = (4 * g) * 512 + 128 * b
                    out_ap = bass.AP(ag_in0 if hh == 0 else ag_in1, off,
                                     [[SEQ, 128], [512, 4], [1, 128]])
                    p.inc(s.dma_start(
                        out=out_ap, in_=AGS[:, (bi % 2) * 512:(bi % 2 + 1) * 512]),
                        "dYA%d%d" % (hh, bi % 2), 16)

        if on("exch"):
            @block.gpsimd
            def _(g):
                p.wait(g, "dYA00", p.cnt["dYA00"])
                p.wait(g, "dYA01", p.cnt["dYA01"])
                p.inc(g.collective_compute(
                    "AllGather", ALU.bypass, replica_groups=[list(range(NC))],
                    ins=[ag_in0.ap().opt()], outs=[ag_out0.ap().opt()]), "cCC")
                p.wait(g, "dYA10", p.cnt["dYA10"])
                p.wait(g, "dYA11", p.cnt["dYA11"])
                p.inc(g.collective_compute(
                    "AllGather", ALU.bypass, replica_groups=[list(range(NC))],
                    ins=[ag_in1.ap().opt()], outs=[ag_out1.ap().opt()]), "cCC")
                p.wait(g, "dG", 16)
                with g.register("r0") as r0:
                    p.wait(g, "cCC", 1)
                    g.reg_load(r0, CIDS[0:1, 0:1])
                    g.reg_mul(r0, r0, 64)
                    for s2 in range(8):
                        in_ap = bass.AP(ag_out0, r0, [[SEQ, 128], [512, 16], [1, 64]])
                        p.inc(g.dma_start(out=GT[:, (2 * s2) * KB:(2 * s2 + 1) * KB],
                                          in_=in_ap), "dG", 16)
                        if s2 < 7:
                            g.reg_add(r0, r0, 128 * SEQ)
                with g.register("r1") as r1:
                    p.wait(g, "cCC", 2)
                    g.reg_load(r1, CIDS[0:1, 0:1])
                    g.reg_mul(r1, r1, 64)
                    for s2 in range(8):
                        in_ap = bass.AP(ag_out1, r1, [[SEQ, 128], [512, 16], [1, 64]])
                        p.inc(g.dma_start(out=GT[:, (2 * s2 + 1) * KB:(2 * s2 + 2) * KB],
                                          in_=in_ap), "dG", 16)
                        if s2 < 7:
                            g.reg_add(r1, r1, 128 * SEQ)

        gt_dG = p.cnt["dG"]

        if on("wo"):
            bW = {k: p.cnt[k] for k in ("sPE", "eDVE", "rB")}
            for j in range(32):
                do, pos = j // 8, j % 8
                if pos < 7:
                    ringB.mark("rB", bW["rB"] + do * 7 + pos + 1)
                else:
                    ringB.mark("sPE", bW["sPE"] + do + 1)

            @block.gpsimd
            def _(g):
                for t in range(32):
                    ringB.load(g, wo_g.ap()[t])

            @block.tensor
            def _(t):
                di_order = [0, 2, 4, 6, 8, 10, 12, 14, 1, 3, 5, 7, 9, 11, 13, 15]
                for do in range(4):
                    if do >= 1:
                        p.wait(t, "eDVE", bW["eDVE"] + 8 * do)
                    for ii, di in enumerate(di_order):
                        p.wait(t, "dG", 16 * (9 if ii < 8 else 17))
                        w = ringB.use(t)
                        for tt in range(8):
                            instr = t.matmul(
                                bank(tt),
                                GT[:, di * KB + 128 * tt:di * KB + 128 * (tt + 1)],
                                w, start=(ii == 0), stop=(ii == 15))
                        if ii % 2 == 1 and ii != 15:
                            p.inc(instr, "rB")
                    p.inc(instr, "sPE")

            @block.vector
            def _(v):
                for do in range(4):
                    p.wait(v, "sPE", bW["sPE"] + do + 1)
                    for tt in range(8):
                        p.inc(v.tensor_copy(
                            X1[:, tt * D + 512 * do:tt * D + 512 * (do + 1)],
                            bank(tt)), "eDVE")

        wo_eDVE = p.cnt["eDVE"]

        if on("ln2"):
            bL2 = {k: p.cnt[k] for k in ("eDVE", "sACT", "sDVE", "sPE")}
            e0, s0 = bL2["eDVE"], bL2["sPE"]

            @block.sync
            def _(s):
                for k in range(8):
                    if k >= 3:
                        p.wait(s, "eDVE", e0 + 6 * (k - 2) - 3)
                    elif k == 2:
                        p.wait(s, "eDVE", e0 + 1)
                    else:
                        p.wait(s, "eDVE", ln1_eDVE)
                    nX[k % 2] += 1
                    p.inc(s.dma_start(out=XST[:, (k % 2) * D:(k % 2 + 1) * D],
                                      in_=xr.ap()[128 * k:128 * (k + 1), :]),
                          "dX%d" % (k % 2), 16)

            for k in range(8):
                x1t = X1[:, k * D:(k + 1) * D]

                @block.vector
                def _(v, k=k, x1t=x1t):
                    p.wait(v, "dX%d" % (k % 2), 16 * (4 + k // 2 + 1))
                    p.wait(v, "eDVE", wo_eDVE - 8 + k + 1)
                    nv = p.inc(v.tensor_tensor(
                        x1t, x1t, XST[:, (k % 2) * D:(k % 2 + 1) * D], ALU.add),
                        "eDVE")
                    assert nv == ((e0 + 1) if k == 0 else (e0 + 6 * k - 3))

                if k >= 1:
                    emit_ln_copies(k - 1, 128 * (k - 1),
                                   dict(sPE0=s0 + 4 * (k - 1),
                                        copies_eDVE=e0 + 6 * (k - 1) + 3))

                @block.sync
                def _(s, k=k, x1t=x1t):
                    p.wait(s, "eDVE", (e0 + 1) if k == 0 else (e0 + 6 * k - 3))
                    p.inc(s.dma_start(out=x1s.ap()[128 * k:128 * (k + 1), :],
                                      in_=x1t), "dSP", 16)

                emit_ln_tile(
                    k, x1t, 128 * k,
                    dict(src_waits=[("eDVE", (e0 + 1) if k == 0 else (e0 + 6 * k - 3))],
                         sq_guard=(e0 + 6 * (k - 2) + 2) if k >= 2 else 0,
                         sACT0=bL2["sACT"] + 2 * k,
                         sDVE_val=bL2["sDVE"] + k + 1,
                         norm_eDVE=e0 + 6 * k + 2,
                         pe_round_waits=[(e0 + 6 * k) if k >= 1 else 0,
                                         (e0 + 6 * k + 1) if k >= 1 else 0,
                                         e0 + 6 * k + 4,
                                         e0 + 6 * k + 5],
                         sPE0=s0 + 4 * k))
            emit_ln_copies(7, 128 * 7, dict(sPE0=s0 + 28, copies_eDVE=e0 + 44))

        ln2_eDVE = p.cnt["eDVE"]
        spill_dSP = p.cnt["dSP"]

        def ffn_12(asrc, dst, bias_sb, nk, wsrc, ngroups):
            b0 = {k: p.cnt[k] for k in ("sPE", "eACT", "rA")}
            gper = nk // 8
            for j in range(ngroups):
                m, pos = j // gper, j % gper
                if pos < gper - 1:
                    ringA.mark("rA", b0["rA"] + m * (gper - 1) + pos + 1)
                else:
                    ringA.mark("sPE", b0["sPE"] + m + 1)

            @block.gpsimd
            def _(g):
                for t in range(ngroups):
                    ringA.load(g, wsrc.ap()[t])

            @block.tensor
            def _(t):
                p.wait(t, "eDVE", ln2_eDVE)
                p.wait(t, "dSP", spill_dSP)
                for m in range(32):
                    pr2 = m % 2
                    p.wait(t, "eACT", b0["eACT"] + max(0, m - 1))
                    for kc in range(nk):
                        w = ringA.use(t)
                        t.matmul(bank(0 + 2 * pr2), w, asrc[:, kc * KB:kc * KB + 512],
                                 start=(kc == 0), stop=(kc == nk - 1))
                        instr = t.matmul(bank(1 + 2 * pr2), w,
                                         asrc[:, kc * KB + 512:(kc + 1) * KB],
                                         start=(kc == 0), stop=(kc == nk - 1))
                        if kc % 8 == 7 and kc != nk - 1:
                            p.inc(instr, "rA")
                    p.inc(instr, "sPE")

            @block.scalar
            def _(sc):
                p.wait(sc, "dSP", spill_dSP)
                p.wait(sc, "dC", 32)
                for m in range(32):
                    pr2 = m % 2
                    p.wait(sc, "sPE", b0["sPE"] + m + 1)
                    sc.activation(dst[:, m * KB:m * KB + 512], bank(0 + 2 * pr2),
                                  AF.Relu, bias=bias_sb[:, m:m + 1])
                    p.inc(sc.activation(dst[:, m * KB + 512:(m + 1) * KB],
                                        bank(1 + 2 * pr2), AF.Relu,
                                        bias=bias_sb[:, m:m + 1]), "eACT")

        if on("ffn1"):
            ffn_12(HT, Z1, B1S, 16, w1_g, 64)
        if on("ffn2"):
            ffn_12(Z1, Z2, B2S, 32, w2_g, 128)

        if on("ffn3"):
            bF3 = {k: p.cnt[k] for k in ("sPE", "eDVE", "dYY", "eACT")}
            nB = [0, 0]
            nXf = [nX[0], nX[1]]
            bF3["rB"] = p.cnt["rB"]
            for j in range(64):
                do, pos = j // 16, j % 16
                if pos < 15:
                    ringB.mark("rB", bF3["rB"] + do * 15 + pos + 1)
                else:
                    ringB.mark("sPE", bF3["sPE"] + do + 1)

            @block.gpsimd
            def _(g):
                for t in range(64):
                    ringB.load(g, w3_g.ap()[t])

            X1R4 = [X1R[:, 0:512], X1R[:, 512:1024], B3S[:, 0:512], B3S[:, 512:1024],
                    XST[:, 1536:2048], X1T0[:], X1T1[:], X1T2[:]]
            B3F = XST[:, 2048:4096]
            FT3 = XST[:, 1024:1536]

            @block.sync
            def _(s):
                p.wait(s, "dSP", spill_dSP)
                p.inc(s.dma_start(out=B3F, in_=b3d.ap()), "dB0", 16)
                q = 0
                for do in range(4):
                    for tt in range(8):
                        if q >= 8:
                            p.wait(s, "eDVE", bF3["eDVE"] + q - 7)
                        p.inc(s.dma_start(
                            out=X1R4[q % 8],
                            in_=x1s.ap()[128 * tt:128 * (tt + 1),
                                         512 * do:512 * (do + 1)]),
                            "dR%d" % (q % 8), 16)
                        if q >= 1:
                            qs = q - 1
                            p.wait(s, "eDVE", bF3["eDVE"] + qs + 1)
                            p.inc(s.dma_start(
                                out=y.ap()[128 * (qs % 8):128 * (qs % 8 + 1),
                                           512 * (qs // 8):512 * (qs // 8 + 1)],
                                in_=YST[:, (qs % 2) * 512:(qs % 2 + 1) * 512]),
                                "dY%d" % (qs % 2), 16)
                        q += 1
                p.wait(s, "eDVE", bF3["eDVE"] + 32)
                p.inc(s.dma_start(
                    out=y.ap()[128 * 7:128 * 8, 512 * 3:512 * 4],
                    in_=YST[:, 512:1024]), "dY1", 16)

            @block.tensor
            def _(t):
                p.wait(t, "eACT", p.cnt["eACT"])
                for do in range(4):
                    if do >= 1:
                        p.wait(t, "eDVE", bF3["eDVE"] + 8 * do)
                    for kc in range(32):
                        w = ringB.use(t)
                        for tt in range(8):
                            instr = t.matmul(
                                bank(tt),
                                Z2[:, kc * KB + 128 * tt:kc * KB + 128 * (tt + 1)],
                                w, start=(kc == 0), stop=(kc == 31))
                        if kc % 2 == 1 and kc != 31:
                            p.inc(instr, "rB")
                    p.inc(instr, "sPE")

            @block.vector
            def _(v):
                q = 0
                p.wait(v, "dB0", 16)
                for do in range(4):
                    p.wait(v, "sPE", bF3["sPE"] + do + 1)
                    for tt in range(8):
                        p.wait(v, "dR%d" % (q % 8), 16 * (q // 8 + 1))
                        if q >= 2:
                            p.wait(v, "dY%d" % (q % 2), 16 * ((q - 2) // 2 + 1))
                        v.tensor_tensor(FT3, bank(tt), X1R4[q % 8], ALU.add)
                        p.inc(v.tensor_tensor(
                            YST[:, (q % 2) * 512:(q % 2 + 1) * 512], FT3,
                            B3F[:, 512 * do:512 * (do + 1)], ALU.add),
                            "eDVE")
                        q += 1

        sb_map = {"hT": HT, "QT": QT, "KT": KT, "V": VSB, "GT": GT, "h2T": HT,
                  "x1": X1, "z1T": Z1, "z2T": Z2, "mh": MH}

        @block.gpsimd
        def _(g):
            for nm in ["eDVE", "eACT", "sACT", "sPE", "aV", "aP", "aL", "dG",
                       "dSP", "dYY", "dY0", "dY1",
                       "dYA00", "dYA01", "dYA10", "dYA11"]:
                p.wait(g, nm, p.cnt[nm])
            for k in debug:
                if k == "agin":
                    p.inc(g.dma_start(out=GT[:, 0:SEQ], in_=ag_in0.ap()), "dYY", 16)
                    p.inc(g.dma_start(out=GT[:, SEQ:2 * SEQ],
                                      in_=ag_in1.ap()), "dYY", 16)
                    p.wait(g, "dYY", p.cnt["dYY"])
                    instr = g.dma_start(out=dbg[k].ap(), in_=GT[:, 0:2 * SEQ])
                else:
                    instr = g.dma_start(out=dbg[k].ap(), in_=sb_map[k][:])
                p.inc(instr, "dYY", 16)
            p.wait(g, "dYY", p.cnt["dYY"])

        @block.sync
        def _(s):
            p.wait(s, "dYY", p.cnt["dYY"])
            p.wait(s, "dY0", p.cnt["dY0"])
            p.wait(s, "dY1", p.cnt["dY1"])

    return nc


_CACHE = {}


def _perm(c):
    t = np.arange(1024)
    return 16 * (64 * c + (t % 64)) + (t // 64)


WO_PERM = [0, 2, 4, 6, 8, 10, 12, 14, 1, 3, 5, 7, 9, 11, 13, 15]


def _pretile(w, kchunks, mchunks, mwidth, group, kperm=None):
    wt = np.ascontiguousarray(np.asarray(w, np.float32).T)
    a = wt.reshape(kchunks, 128, mchunks, mwidth).transpose(2, 0, 1, 3)
    tiles = a.reshape(mchunks * kchunks, 128, mwidth)
    if kperm is not None:
        tiles = tiles.reshape(mchunks, kchunks, 128, mwidth)[:, kperm]
        tiles = tiles.reshape(mchunks * kchunks, 128, mwidth)
    gshape = (tiles.shape[0] // group, group, 128, mwidth)
    gg = tiles.reshape(gshape).transpose(0, 2, 1, 3)
    return np.ascontiguousarray(gg.reshape(gshape[0], 128, group * mwidth))


def kernel(L0_attn_x, L0_attn_Wq, L0_attn_Wk, L0_attn_Wv, L0_attn_Wo,
           L0_ffn_W1, L0_ffn_b1, L0_ffn_W2, L0_ffn_b2, L0_ffn_W3, L0_ffn_b3):
    x = np.asarray(L0_attn_x, dtype=np.float32)[0]
    if "nc" not in _CACHE:
        _CACHE["nc"] = build_program()
    nc = _CACHE["nc"]
    wq = _pretile(L0_attn_Wq, 16, 16, 128, 8)
    wk = _pretile(L0_attn_Wk, 16, 16, 128, 8)
    wv = _pretile(L0_attn_Wv, 16, 4, 512, 2)
    wo = _pretile(L0_attn_Wo, 16, 4, 512, 2, WO_PERM)
    w1 = _pretile(L0_ffn_W1, 16, 32, 128, 8)
    w2 = _pretile(L0_ffn_W2, 32, 32, 128, 8)
    w3 = _pretile(L0_ffn_W3, 32, 4, 512, 2)
    b1 = np.ascontiguousarray(np.asarray(L0_ffn_b1, np.float32).reshape(32, 128).T)
    b2 = np.ascontiguousarray(np.asarray(L0_ffn_b2, np.float32).reshape(32, 128).T)
    b3 = np.ascontiguousarray(
        np.broadcast_to(np.asarray(L0_ffn_b3, np.float32), (128, D)))
    in_maps, perms = [], []
    for c in range(NC):
        pc = _perm(c)
        perms.append(pc)
        in_maps.append(dict(
            xq=np.ascontiguousarray(x[1024 * c:1024 * (c + 1)]),
            xr=np.ascontiguousarray(x[pc]),
            cid=np.array([[c]], dtype=np.int32),
            wq_g=wq, wk_g=wk, wv_g=wv, wo_g=wo, w1_g=w1, w2_g=w2, w3_g=w3,
            b1d=b1, b2d=b2, b3d=b3))
    res = run_bass_kernel_spmd(nc, in_maps, core_ids=list(range(NC)))
    out = np.empty((SEQ, D), dtype=np.float32)
    for c in range(NC):
        out[perms[c]] = res.results[c]["y"]
    return out.reshape(1, SEQ, D)
